# version 05
# baseline (speedup 1.0000x reference)
import numpy as np
import ml_dtypes
import concourse.bass as bass
import concourse.mybir as mybir
from concourse.bass_utils import run_bass_kernel_spmd

F32 = mybir.dt.float32
BF16 = mybir.dt.bfloat16
AF = mybir.ActivationFunctionType
ALU = mybir.AluOpType
AX = mybir.AxisListType

S = 2048
D = 2048
NKH, NVH, DK, DV = 16, 32, 128, 128
KEY_DIM, VAL_DIM = NKH * DK, NVH * DV
FQKVZ = 2 * KEY_DIM + 2 * VAL_DIM
CONV_DIM = 2 * KEY_DIM + VAL_DIM
HID = 8192
NU = 16
UT = 128
LNEPS = 1e-5
GEPS = 1e-6
RMSEPS = float(np.finfo(np.float32).eps)
LNSQ = float(np.log(np.sqrt(128.0)))
BIGM = 30000.0
bf = ml_dtypes.bfloat16


class Buf:
    __slots__ = ("name", "writer", "readers")
    def __init__(self, name):
        self.name = name
        self.writer = None
        self.readers = []


class Prog:
    def __init__(self):
        self.engines = {}
        self.dstreams = {}
        self.mksem = None

    def add_engine(self, name, sem):
        self.engines[name] = {"ops": [], "count": 0, "hw": {}, "sem": sem}

    def dstream(self, key):
        if key not in self.dstreams:
            self.dstreams[key] = {"sem": self.mksem("dma_" + key), "count": 0}
        return self.dstreams[key]

    def op(self, eng, fn, reads=(), writes=(), dma=False, dkey=None):
        e = self.engines[eng]
        waits = []
        def need(tok):
            if tok is None:
                return
            sem, val, src = tok
            if val is None:
                val = src["count"]
            hw = e["hw"].get(id(sem), 0)
            if val > hw:
                waits.append((sem, val))
                e["hw"][id(sem)] = val
        for b in reads:
            need(b.writer)
        for b in writes:
            need(b.writer)
            for tok in b.readers:
                need(tok)
        if dma:
            if dkey is None:
                dkey = writes[0].name if writes else reads[0].name
            st = self.dstream(dkey)
            st["count"] += 16
            tok = (st["sem"], None, st)
            inc_sem, inc = st["sem"], 16
        else:
            e["count"] += 1
            tok = (e["sem"], e["count"], e)
            inc_sem, inc = e["sem"], 1
        for b in writes:
            b.writer = tok
            b.readers = []
        for b in reads:
            b.readers.append(tok)
        e["ops"].append((fn, waits, inc, inc_sem))

    def replay(self, block):
        for name, e in self.engines.items():
            deco = getattr(block, name)
            ops = e["ops"]
            def make(ops=ops):
                def body(q):
                    for fn, waits, inc, inc_sem in ops:
                        for s, v in waits:
                            q.wait_ge(s, v)
                        inst = fn(q)
                        if inst is not None:
                            inst.then_inc(inc_sem, inc)
                return body
            deco(make())


def build_program():
    nc = bass.Bass("TRN2", target_bir_lowering=False)

    xT = nc.dram_tensor("xT", [D, S], F32, kind="ExternalInput").ap()
    Wqkvz = nc.dram_tensor("Wqkvz", [D, FQKVZ], F32, kind="ExternalInput").ap()
    Wba = nc.dram_tensor("Wba", [D, 64], F32, kind="ExternalInput").ap()
    convkT = nc.dram_tensor("convkT", [CONV_DIM, 4], F32, kind="ExternalInput").ap()
    Alog = nc.dram_tensor("Alog", [32, 1], F32, kind="ExternalInput").ap()
    dtbI = nc.dram_tensor("dtbI", [32, 1], F32, kind="ExternalInput").ap()
    nwIn = nc.dram_tensor("nwIn", [1, DV], F32, kind="ExternalInput").ap()
    Wout = nc.dram_tensor("Wout", [VAL_DIM, D], F32, kind="ExternalInput").ap()
    Wfc = nc.dram_tensor("Wfc", [HID, D], F32, kind="ExternalInput").ap()
    Wproj = nc.dram_tensor("Wproj", [D, HID], F32, kind="ExternalInput").ap()
    outT = nc.dram_tensor("outT", [D, S], F32, kind="ExternalOutput").ap()

    spill = nc.dram_tensor("spill", [FQKVZ, S], BF16, kind="Internal").ap()
    w1t_s = nc.dram_tensor("w1t_s", [D, HID], BF16, kind="Internal").ap()
    w2t_s = nc.dram_tensor("w2t_s", [HID, D], BF16, kind="Internal").ap()
    wout_s = nc.dram_tensor("wout_s", [VAL_DIM, D], BF16, kind="Internal").ap()
    x1f_s = nc.dram_tensor("x1f_s", [D, S], F32, kind="Internal").ap()
    dbg_xf = nc.dram_tensor("dbg_xf", [VAL_DIM, S], BF16, kind="Internal").ap()
    dbg_at = nc.dram_tensor("dbg_at", [D, S], BF16, kind="Internal").ap()
    x1b_s = nc.dram_tensor("x1b_s", [D, S], BF16, kind="Internal").ap()

    I128b_t = nc.inline_tensor(np.eye(128, dtype=bf), name="c_I128b")
    onesColB_t = nc.inline_tensor(np.ones((128, 1), dtype=bf), name="c_onescolb")
    onesRowF_t = nc.inline_tensor(np.ones((1, 128), dtype=np.float32), name="c_onesrowf")
    ti = np.arange(128)
    mL = np.where(ti[:, None] > ti[None, :], 0.0, BIGM).astype(bf)
    mU = np.where(ti[None, :] > ti[:, None], 0.0, -BIGM).astype(bf)
    mI = np.where(ti[None, :] >= ti[:, None], 0.0, -BIGM).astype(bf)
    M_L_t = nc.inline_tensor(mL.astype(np.float32), name="c_mL")
    M_U_t = nc.inline_tensor(mU.astype(np.float32), name="c_mU")
    M_I_t = nc.inline_tensor(mI.astype(np.float32), name="c_mI")

    from contextlib import ExitStack
    es = ExitStack()
    p = Prog()
    B = {}
    def buf(name):
        if name not in B:
            B[name] = Buf(name)
        return B[name]

    with es:
        sems = {}
        p.mksem = lambda nm: es.enter_context(nc.semaphore(nm))
        for n in ["sync", "tensor", "vector", "scalar", "gpsimd"]:
            sems[n] = es.enter_context(nc.semaphore("sem_" + n))
            p.add_engine(n, sems[n])

        def sb(name, shape, dtype):
            return es.enter_context(nc.sbuf_tensor(name, shape, dtype))

        consts = sb("consts", [128, 128], BF16)
        cI = consts[:, 0:128]
        cmask = sb("cmask", [128, 128 * 3], F32)
        cML = cmask[:, 0:128]
        cMU = cmask[:, 128:256]
        cMI = cmask[:, 256:384]
        cOnesColB = sb("conescolb", [128, 1], BF16)
        cOnesColF = sb("conescolf", [128, 1], F32)
        cOnesRowF = sb("conesrowf", [1, 128], F32)
        cNW = sb("cnw", [128, 128], F32)
        cIF = sb("cif", [128, 128], F32)
        R1B = sb("R1B", [128, S], F32)
        RXB = sb("RXB", [128, 1024], F32)
        rows = sb("rows", [1, 3 * S + 2688], F32)
        rows_b = sb("rows_b", [1, S], BF16)
        grow = sb("grow", [32, S * 3], F32)
        g_gc = grow[:, 0:S]
        g_lnb = grow[:, S:2 * S]
        g_scr = grow[:, 2 * S:3 * S]
        rstg = sb("rstg", [1, 256], F32)
        colsA = sb("colsA", [32, 4], F32)
        CC = sb("CC", [128, 8], F32)
        CCvals = [LNEPS, GEPS, RMSEPS, 1.0, -1.0, -LNSQ, 0.0883883476]
        def cb(v):
            return CC[:, CCvals.index(v):CCvals.index(v) + 1]
        def cb1(v):
            return CC[0:1, CCvals.index(v):CCvals.index(v) + 1]
        wbaB = sb("wbaB", [128, 16 * 64], F32)
        bacs = sb("bacs", [1, 64], F32)
        baT = None
        qcs = sb("qcs", [1, 256], BF16)
        colsK = sb("colsK", [128, 4], F32)
        dstage = sb("dstage", [128, 4 * 128], BF16)
        wts = sb("wts", [128, 128 * 4], F32)
        wtsb = sb("wtsb", [128, 128], BF16)
        stk = sb("stk", [128, 128], F32)
        scol = sb("scol", [128, 4], F32)
        eglmg = sb("eglmg", [128, 32], F32)
        egbl_n = sb("egbl_n", [128, 32], F32)
        egcq = sb("egcq", [128, 32], F32)
        eglc = sb("eglc", [128, 32], F32)
        betac = sb("betac", [128, 32], F32)
        gblc = sb("gblc", [128, 32], F32)
        neggc = sb("neggc", [128, 32], F32)
        neggcq = sb("neggcq", [128, 32], F32)

        AR = sb("arena", [128, 50592], BF16)
        def arf32(lo_bytes, n):
            return AR[:, lo_bytes // 2: lo_bytes // 2 + 2 * n].bitcast(F32)
        def arb16(lo_bytes, n):
            return AR[:, lo_bytes // 2: lo_bytes // 2 + n]
        baT = arf32(90112, S)[0:64, :]

        PB = [es.enter_context(nc.psum_tensor("pb%d" % i, [128, 512], F32)) for i in range(8)]
        pb = [buf("pb%d" % i) for i in range(8)]

        for _j, _v in enumerate(CCvals):
            p.op("vector", lambda q, j=_j, v=_v: q.memset(CC[:, j:j + 1], v), writes=[buf("cc")])
        p.op("vector", lambda q: q.memset(cOnesColF[:], 1.0), writes=[buf("onescolf")])
        p.op("sync", lambda q: q.dma_start(out=cI, in_=I128b_t.ap()),
             writes=[buf("consts")], dma=True, dkey="cinit")
        for dst, srct in [(cML, M_L_t), (cMU, M_U_t), (cMI, M_I_t)]:
            p.op("sync", lambda q, d=dst, s=srct: q.dma_start(out=d, in_=s.ap()),
                 writes=[buf("cmask")], dma=True, dkey="cinit")
        p.op("sync", lambda q: q.dma_start(out=cOnesColB[:], in_=onesColB_t.ap()),
             writes=[buf("onescolb")], dma=True, dkey="cinit")
        p.op("sync", lambda q: q.dma_start(out=cOnesRowF[:], in_=onesRowF_t.ap()),
             writes=[buf("onesrowf")], dma=True, dkey="cinit")
        p.op("sync", lambda q: q.dma_start(out=colsA[:, 2:3], in_=Alog),
             writes=[buf("colsA")], dma=True, dkey="cinit")
        p.op("sync", lambda q: q.dma_start(out=colsA[:, 1:2], in_=dtbI),
             writes=[buf("colsA")], dma=True, dkey="cinit")
        nwrow = rows[0:1, 3 * S + 2560:3 * S + 2688]
        p.op("sync", lambda q: q.dma_start(out=nwrow, in_=nwIn),
             writes=[buf("nwrow")], dma=True, dkey="cinit")
        p.op("tensor", lambda q: q.matmul(PB[5][:, 0:128], cOnesRowF[:], nwrow, start=True, stop=True),
             reads=[buf("onesrowf"), buf("nwrow")], writes=[pb[5]])
        p.op("vector", lambda q: q.tensor_copy(cNW[:], PB[5][:, 0:128]),
             reads=[pb[5]], writes=[buf("cnw")])
        p.op("vector", lambda q: q.tensor_copy(cIF[:], cI),
             reads=[buf("consts")], writes=[buf("cif")])
        p.op("scalar", lambda q: q.activation(colsA[:, 3:4], colsA[:, 2:3], AF.Exp),
             reads=[buf("colsA")], writes=[buf("colsA")])
        p.op("vector", lambda q: q.tensor_scalar(out=colsA[:, 0:1], in0=colsA[:, 3:4],
                                                 scalar1=-1.0, scalar2=None, op0=ALU.mult),
             reads=[buf("colsA")], writes=[buf("colsA")])

        xTb = arb16(0, 16 * S)
        xstage = arf32(65536, S)
        xsqb = arb16(94336, S)
        for kt in range(16):
            bx = buf("xstage")
            p.op("sync", lambda q, kt=kt: q.dma_start(out=xstage, in_=xT[kt * 128:(kt + 1) * 128, :]),
                 writes=[bx], dma=True)
            p.op("vector", lambda q, kt=kt: q.tensor_copy(xTb[:, kt * S:(kt + 1) * S], xstage),
                 reads=[bx], writes=[buf("xTb")])
            p.op("scalar", lambda q: q.activation(xsqb, xstage, AF.Square),
                 reads=[bx], writes=[buf("xsqb")])
            for nb in range(4):
                p.op("tensor", lambda q, kt=kt, nb=nb: q.matmul(
                        PB[nb][0:1, 0:512], cOnesColB[:],
                        xTb[:, kt * S + nb * 512:kt * S + (nb + 1) * 512],
                        start=(kt == 0), stop=(kt == 15)),
                     reads=[buf("xTb"), buf("onescolb")], writes=[pb[nb]])
                p.op("tensor", lambda q, kt=kt, nb=nb: q.matmul(
                        PB[4 + nb][0:1, 0:512], cOnesColB[:],
                        xsqb[:, nb * 512:(nb + 1) * 512],
                        start=(kt == 0), stop=(kt == 15)),
                     reads=[buf("xsqb"), buf("onescolb")], writes=[pb[4 + nb]])

        r_rstd1 = rows[0:1, 0:S]
        ln1scr = arf32(73728, 3 * S)[0:1, :]
        r_negmr1 = ln1scr[0:1, 2 * S:3 * S]
        rb_negmr1 = rows_b[0:1, 0:S]

        def rowstats(sumaps, sqaps, width, rstd_dst, negmr_dst, negmr_b_dst, eps, nfeat,
                     rstd_mult=True, scr=None, scr_buf="rows"):
            if scr is None:
                scr = rows[0:1, 3 * S:3 * S + 2560]
            mrow = scr[0:1, 0:width]
            exrow = scr[0:1, width:2 * width]
            vrow = scr[0:1, 2 * width:3 * width]
            sbuf_scr = buf(scr_buf)
            off = 0
            for (a, bq), (a2, bq2) in zip(sumaps, sqaps):
                w = a.shape[-1]
                p.op("vector", lambda q, a=a, off=off, w=w: q.tensor_scalar(
                        out=mrow[0:1, off:off + w], in0=a,
                        scalar1=1.0 / nfeat, scalar2=None, op0=ALU.mult),
                     reads=[bq], writes=[sbuf_scr])
                p.op("vector", lambda q, a2=a2, off=off, w=w: q.tensor_scalar(
                        out=exrow[0:1, off:off + w], in0=a2,
                        scalar1=1.0 / nfeat, scalar2=None, op0=ALU.mult),
                     reads=[bq2], writes=[sbuf_scr])
                off += w
            p.op("vector", lambda q: q.tensor_tensor(out=vrow, in0=mrow, in1=mrow, op=ALU.mult),
                 reads=[sbuf_scr], writes=[sbuf_scr])
            p.op("vector", lambda q: q.tensor_tensor(out=vrow, in0=exrow, in1=vrow, op=ALU.subtract),
                 reads=[sbuf_scr], writes=[sbuf_scr])
            p.op("scalar", lambda q: q.activation(vrow, vrow, AF.Sqrt, bias=cb1(eps)),
                 reads=[sbuf_scr], writes=[sbuf_scr])
            WS = [sbuf_scr, buf("rows"), buf("rows_b")]
            p.op("vector", lambda q: q.reciprocal(rstd_dst, vrow), reads=[sbuf_scr], writes=WS)
            if rstd_mult:
                p.op("vector", lambda q: q.tensor_tensor(out=negmr_dst, in0=mrow, in1=rstd_dst, op=ALU.mult),
                     reads=[sbuf_scr] + WS, writes=WS)
                p.op("vector", lambda q: q.tensor_scalar(out=negmr_dst, in0=negmr_dst, scalar1=-1.0, scalar2=None, op0=ALU.mult),
                     reads=WS, writes=WS)
            else:
                p.op("vector", lambda q: q.tensor_scalar(out=negmr_dst, in0=mrow, scalar1=-1.0, scalar2=None, op0=ALU.mult),
                     reads=[sbuf_scr] + WS, writes=WS)
            if negmr_b_dst is not None:
                p.op("vector", lambda q: q.tensor_scalar(out=negmr_b_dst, in0=mrow, scalar1=-1.0, scalar2=None, op0=ALU.mult),
                     reads=[sbuf_scr] + WS, writes=WS)

        rowstats([(PB[nb][0:1, 0:512], pb[nb]) for nb in range(4)],
                 [(PB[4 + nb][0:1, 0:512], pb[4 + nb]) for nb in range(4)],
                 S, r_rstd1, r_negmr1, rb_negmr1, LNEPS, D, rstd_mult=False,
                 scr=ln1scr[0:1, 0:3 * S], scr_buf="ln1scr")

        def bcast_row(dst, src_row, tag, width):
            for c0 in range(0, width, 512):
                w = min(512, width - c0)
                p.op("tensor", lambda q, c0=c0, w=w: q.matmul(
                        PB[5][:, 0:w], cOnesRowF[:], src_row[0:1, c0:c0 + w], start=True, stop=True),
                     reads=[buf("rows"), buf("onesrowf")], writes=[pb[5]])
                p.op("vector", lambda q, c0=c0, w=w: q.tensor_copy(dst[:, c0:c0 + w], PB[5][:, 0:w]),
                     reads=[pb[5]], writes=[buf(tag)])
        bcast_row(R1B, r_rstd1, "R1B", S)

        for kt in range(16):
            p.op("sync", lambda q, kt=kt: q.dma_start(out=wbaB[:, kt * 64:(kt + 1) * 64],
                                                      in_=Wba[kt * 128:(kt + 1) * 128, :]),
                 writes=[buf("wbaB")], dma=True)
        for kt in range(16):
            p.op("tensor", lambda q, kt=kt: q.matmul(
                    PB[0][0:1, 0:64], cOnesColF[:], wbaB[:, kt * 64:(kt + 1) * 64],
                    start=(kt == 0), stop=(kt == 15)),
                 reads=[buf("wbaB"), buf("onescolf")], writes=[pb[0]])
        p.op("vector", lambda q: q.tensor_copy(bacs[:], PB[0][0:1, 0:64]),
             reads=[pb[0]], writes=[buf("bacs")])
        for kt in range(16):
            p.op("sync", lambda q, kt=kt: q.dma_start(out=xstage, in_=xT[kt * 128:(kt + 1) * 128, :]),
                 writes=[buf("xstage")], dma=True)
            for nb in range(4):
                p.op("tensor", lambda q, kt=kt, nb=nb: q.matmul(
                        PB[4 + nb][0:64, 0:512], wbaB[:, kt * 64:(kt + 1) * 64],
                        xstage[:, nb * 512:(nb + 1) * 512],
                        start=(kt == 0), stop=False),
                     reads=[buf("wbaB"), buf("xstage")], writes=[pb[4 + nb]])
        for nb in range(4):
            p.op("tensor", lambda q, nb=nb: q.matmul(
                    PB[4 + nb][0:64, 0:512], bacs[0:1, :],
                    r_negmr1[0:1, nb * 512:(nb + 1) * 512], start=False, stop=True),
                 reads=[buf("bacs"), buf("ln1scr")], writes=[pb[4 + nb]])
            p.op("vector", lambda q, nb=nb: q.tensor_tensor(
                    out=baT[:, nb * 512:(nb + 1) * 512], in0=PB[4 + nb][0:64, 0:512],
                    in1=R1B[0:64, nb * 512:(nb + 1) * 512], op=ALU.mult),
                 reads=[pb[4 + nb], buf("R1B")], writes=[buf("baT")])
        bg = baT[0:32, :]
        ag = baT[32:64, :]
        p.op("scalar", lambda q: q.activation(g_scr, bg, AF.Abs),
             reads=[buf("baT")], writes=[buf("grow")])
        p.op("scalar", lambda q: q.activation(g_scr, g_scr, AF.Exp, scale=cb(-1.0)[0:32, :]),
             reads=[buf("grow")], writes=[buf("grow")])
        p.op("scalar", lambda q: q.activation(g_scr, g_scr, AF.Ln, bias=cb(1.0)[0:32, :]),
             reads=[buf("grow")], writes=[buf("grow")])
        p.op("vector", lambda q: q.tensor_scalar(out=g_lnb, in0=bg, scalar1=0.0, scalar2=None, op0=ALU.min),
             reads=[buf("baT")], writes=[buf("grow")])
        p.op("vector", lambda q: q.tensor_tensor(out=g_lnb, in0=g_lnb, in1=g_scr, op=ALU.subtract),
             reads=[buf("grow")], writes=[buf("grow")])
        p.op("scalar", lambda q: q.activation(g_scr, ag, AF.Abs, bias=colsA[0:32, 1:2]),
             reads=[buf("baT"), buf("colsA")], writes=[buf("grow")])
        p.op("scalar", lambda q: q.activation(g_scr, g_scr, AF.Exp, scale=cb(-1.0)[0:32, :]),
             reads=[buf("grow")], writes=[buf("grow")])
        p.op("scalar", lambda q: q.activation(g_scr, g_scr, AF.Ln, bias=cb(1.0)[0:32, :]),
             reads=[buf("grow")], writes=[buf("grow")])
        p.op("scalar", lambda q: q.activation(g_gc, ag, AF.Relu, bias=colsA[0:32, 1:2]),
             reads=[buf("baT"), buf("colsA")], writes=[buf("grow")])
        p.op("vector", lambda q: q.tensor_tensor(out=g_gc, in0=g_gc, in1=g_scr, op=ALU.add),
             reads=[buf("grow")], writes=[buf("grow")])
        p.op("vector", lambda q: q.tensor_scalar(out=g_gc, in0=g_gc, scalar1=colsA[0:32, 0:1], scalar2=None, op0=ALU.mult),
             reads=[buf("grow"), buf("colsA")], writes=[buf("grow")])
        p.op("vector", lambda q: q.memset(g_scr[:, 0:UT], 0.0), writes=[buf("grow")])
        for u in range(NU):
            p.op("vector", lambda q, u=u: q.tensor_tensor_scan(
                    out=g_gc[:, u * UT:(u + 1) * UT], data0=g_gc[:, u * UT:(u + 1) * UT],
                    data1=g_scr[:, 0:UT], initial=0.0, op0=ALU.add, op1=ALU.add),
                 reads=[buf("grow")], writes=[buf("grow")])
        g_gbl = g_lnb
        p.op("vector", lambda q: q.tensor_tensor(out=g_gbl, in0=g_gc, in1=g_lnb, op=ALU.add),
             reads=[buf("grow")], writes=[buf("grow")])


        wq0 = arb16(73728, 16 * 256)
        wq1 = arb16(81920, 16 * 256)
        convst = arb16(90112, 2051)
        spst = arb16(94336, S)
        for fb in range(48):
            wq = wq0 if fb % 2 == 0 else wq1
            wqb_ = buf("wq%d" % (fb % 2))
            F0 = fb * 256
            for kt in range(16):
                p.op("sync", lambda q, kt=kt, F0=F0: q.dma_start(
                        out=xstage[:, 0:256], in_=Wqkvz[kt * 128:(kt + 1) * 128, F0:F0 + 256]),
                     writes=[buf("xstage")], dma=True)
                p.op("vector", lambda q, kt=kt, wq=wq: q.tensor_copy(
                        wq[:, kt * 256:(kt + 1) * 256], xstage[:, 0:256]),
                     reads=[buf("xstage")], writes=[wqb_])
            for kt in range(16):
                p.op("tensor", lambda q, kt=kt, wq=wq: q.matmul(
                        PB[5][0:1, 0:256], cOnesColB[:], wq[:, kt * 256:(kt + 1) * 256],
                        start=(kt == 0), stop=(kt == 15)),
                     reads=[wqb_, buf("onescolb")], writes=[pb[5]])
            p.op("vector", lambda q: q.tensor_copy(qcs[:], PB[5][0:1, 0:256]),
                 reads=[pb[5]], writes=[buf("qcs")])
            for ft in range(2):
                ftile = fb * 2 + ft
                r0c = (ftile * 128) % 768
                kh = (ftile * 128) // 768
                if r0c == 0:
                    srow = kh * 128
                    isconv = True
                elif r0c == 128:
                    srow = 2048 + kh * 128
                    isconv = True
                elif r0c < 512:
                    srow = 4096 + kh * 256 + (r0c - 256)
                    isconv = True
                else:
                    srow = 8192 + kh * 256 + (r0c - 512)
                    isconv = False
                if isconv:
                    p.op("sync", lambda q, srow=srow: q.dma_start(
                            out=colsK[:, 0:4], in_=convkT[srow:srow + 128, :]),
                         writes=[buf("colsK")], dma=True)
                    for j in range(4):
                        p.op("vector", lambda q, j=j: q.tensor_scalar(
                                out=dstage[:, j * 128:(j + 1) * 128], in0=cI,
                                scalar1=colsK[:, j:j + 1], scalar2=None, op0=ALU.mult),
                             reads=[buf("colsK"), buf("consts")], writes=[buf("dstage")])
                    p.op("vector", lambda q: q.memset(convst[:, 0:3], 0.0),
                         writes=[buf("convst")])
                for nb in range(4):
                    pbq = nb % 2
                    for kt in range(16):
                        p.op("tensor", lambda q, kt=kt, nb=nb, ft=ft, wq=wq, pbq=pbq: q.matmul(
                                PB[pbq][:, 0:512], wq[:, kt * 256 + ft * 128: kt * 256 + ft * 128 + 128],
                                xTb[:, kt * S + nb * 512:kt * S + (nb + 1) * 512],
                                start=(kt == 0), stop=False),
                             reads=[wqb_, buf("xTb")], writes=[pb[pbq]])
                    p.op("tensor", lambda q, nb=nb, ft=ft, pbq=pbq: q.matmul(
                            PB[pbq][:, 0:512], qcs[0:1, ft * 128:ft * 128 + 128],
                            rows_b[0:1, nb * 512:(nb + 1) * 512], start=False, stop=True),
                         reads=[buf("qcs"), buf("rows_b")], writes=[pb[pbq]])
                    if isconv:
                        p.op("vector", lambda q, nb=nb, pbq=pbq: q.tensor_tensor(
                                out=convst[:, 3 + nb * 512:3 + (nb + 1) * 512],
                                in0=PB[pbq][:, 0:512], in1=R1B[:, nb * 512:(nb + 1) * 512], op=ALU.mult),
                             reads=[pb[pbq], buf("R1B")], writes=[buf("convst")])
                    else:
                        p.op("vector", lambda q, nb=nb, pbq=pbq: q.tensor_tensor(
                                out=spst[:, nb * 512:(nb + 1) * 512],
                                in0=PB[pbq][:, 0:512], in1=R1B[:, nb * 512:(nb + 1) * 512], op=ALU.mult),
                             reads=[pb[pbq], buf("R1B")], writes=[buf("spst")])
                if isconv:
                    for nb in range(4):
                        pbc = 2 + nb % 2
                        for j in range(4):
                            p.op("tensor", lambda q, nb=nb, j=j, pbc=pbc: q.matmul(
                                    PB[pbc][:, 0:512], dstage[:, j * 128:(j + 1) * 128],
                                    convst[:, j + nb * 512: j + nb * 512 + 512],
                                    start=(j == 0), stop=(j == 3)),
                                 reads=[buf("dstage"), buf("convst")], writes=[pb[pbc]])
                        p.op("scalar", lambda q, nb=nb, pbc=pbc: q.activation(
                                spst[:, nb * 512:(nb + 1) * 512], PB[pbc][:, 0:512], AF.Silu),
                             reads=[pb[pbc]], writes=[buf("spst")])
                p.op("sync", lambda q, srow=srow: q.dma_start(
                        out=spill[srow:srow + 128, :], in_=spst),
                     reads=[buf("spst")], writes=[buf("spill")], dma=True)

        wtf = arf32(65536, 2048)
        wtb = arb16(73728, 2048)
        wta = arb16(77824, 2048)
        wtq = arb16(81920, 2048)
        wtt = arb16(86016, 2048)
        def ternarize(src, dst, nrow_tiles, ncol, tag, cs_dst):
            CW = 2048
            ng = CW // 64
            ncolt = CW // 128
            nch = ncol // CW
            for rt in range(nrow_tiles):
                for cc in range(nch):
                    bsrc = buf("wtf")
                    p.op("sync", lambda q, rt=rt, cc=cc: q.dma_start(
                            out=wtf[:, 0:CW],
                            in_=src[rt * 128:(rt + 1) * 128, cc * CW:(cc + 1) * CW]),
                         reads=[buf("spill")], writes=[bsrc], dma=True)
                    p.op("vector", lambda q: q.tensor_copy(wtb[:, 0:CW], wtf[:, 0:CW]),
                         reads=[bsrc], writes=[buf("wtb")])
                    p.op("vector", lambda q: q.tensor_reduce(
                            out=wts[:, 0:ng].rearrange("p (g one) -> p g one", one=1),
                            in_=wtb[:, 0:CW].rearrange("p (g e) -> p g e", e=64),
                            axis=AX.X, op=ALU.add, apply_absolute_value=True),
                         reads=[buf("wtb")], writes=[buf("wts")])
                    p.op("vector", lambda q: q.tensor_scalar(
                            out=wts[:, 128:128 + ng], in0=wts[:, 0:ng], scalar1=1.0 / 64,
                            scalar2=1e-8, op0=ALU.mult, op1=ALU.max),
                         reads=[buf("wts")], writes=[buf("wts")])
                    p.op("vector", lambda q: q.tensor_copy(wtsb[:, 0:ng], wts[:, 128:128 + ng]),
                         reads=[buf("wts")], writes=[buf("wtsb")])
                    p.op("vector", lambda q: q.tensor_copy(wts[:, 128:128 + ng], wtsb[:, 0:ng]),
                         reads=[buf("wtsb")], writes=[buf("wts")])
                    p.op("vector", lambda q: q.tensor_scalar(out=wts[:, 256:256 + ng], in0=wts[:, 128:128 + ng],
                                        scalar1=0.5, scalar2=None, op0=ALU.mult),
                         reads=[buf("wts")], writes=[buf("wts")])
                    p.op("vector", lambda q: q.tensor_scalar(out=wts[:, 384:384 + ng], in0=wts[:, 256:256 + ng],
                                        scalar1=-1.0, scalar2=None, op0=ALU.mult),
                         reads=[buf("wts")], writes=[buf("wts")])
                    def terna(q):
                        wv = wtb[:, 0:CW].rearrange("p (g e) -> p g e", e=64)
                        Hv = wts[:, 256:256 + ng].rearrange("p (g one) -> p g one", one=1)
                        return q.tensor_tensor(out=wta[:, 0:CW].rearrange("p (g e) -> p g e", e=64),
                                        in0=wv, in1=Hv.broadcast_to([128, ng, 64]), op=ALU.is_gt)
                    p.op("vector", terna, reads=[buf("wtb"), buf("wts")], writes=[buf("wta")])
                    def ternb(q):
                        wv = wtb[:, 0:CW].rearrange("p (g e) -> p g e", e=64)
                        nHv = wts[:, 384:384 + ng].rearrange("p (g one) -> p g one", one=1)
                        return q.tensor_tensor(out=wtq[:, 0:CW].rearrange("p (g e) -> p g e", e=64),
                                        in0=wv, in1=nHv.broadcast_to([128, ng, 64]), op=ALU.is_lt)
                    p.op("vector", ternb, reads=[buf("wtb"), buf("wts")], writes=[buf("wtq")])
                    p.op("vector", lambda q: q.tensor_tensor(
                            out=wta[:, 0:CW], in0=wta[:, 0:CW], in1=wtq[:, 0:CW], op=ALU.subtract),
                         reads=[buf("wta"), buf("wtq")], writes=[buf("wta")])
                    def ternc(q):
                        sv = wts[:, 128:128 + ng].rearrange("p (g one) -> p g one", one=1)
                        return q.tensor_tensor(out=wtq[:, 0:CW].rearrange("p (g e) -> p g e", e=64),
                                        in0=wta[:, 0:CW].rearrange("p (g e) -> p g e", e=64),
                                        in1=sv.broadcast_to([128, ng, 64]), op=ALU.mult)
                    p.op("vector", ternc, reads=[buf("wta"), buf("wts")], writes=[buf("wtq")])
                    for ct in range(ncolt):
                        pbt = 6 + ct % 2
                        p.op("tensor", lambda q, ct=ct, pbt=pbt: q.matmul(
                                PB[pbt][:, 0:128], wtq[:, ct * 128:(ct + 1) * 128], cI,
                                start=True, stop=True),
                             reads=[buf("wtq"), buf("consts")], writes=[pb[pbt]])
                        p.op("scalar", lambda q, ct=ct, pbt=pbt: q.activation(
                                wtt[:, ct * 128:(ct + 1) * 128], PB[pbt][:, 0:128], AF.Copy),
                             reads=[pb[pbt]], writes=[buf("wtt")])
                    p.op("sync", lambda q, rt=rt, cc=cc: q.dma_start(
                            out=dst.rearrange("(a pp) r -> pp a r", pp=128)[
                                :, cc * ncolt:(cc + 1) * ncolt, rt * 128:(rt + 1) * 128],
                            in_=wtt[:, 0:CW].rearrange("p (a c) -> p a c", c=128)),
                         reads=[buf("wtt")], writes=[buf(tag)], dma=True)
        ternarize(Wfc, w1t_s, 64, 2048, "w1t", None)
        ternarize(Wproj, w2t_s, 16, 8192, "w2t", None)
        for rt in range(32):
            p.op("sync", lambda q, rt=rt: q.dma_start(out=wtf[:, 0:2048],
                                                      in_=Wout[rt * 128:(rt + 1) * 128, :]),
                 writes=[buf("wtf")], dma=True)
            p.op("vector", lambda q: q.tensor_copy(wtb[:, 0:2048], wtf[:, 0:2048]),
                 reads=[buf("wtf")], writes=[buf("wtb")])
            p.op("sync", lambda q, rt=rt: q.dma_start(out=wout_s[rt * 128:(rt + 1) * 128, :],
                                                      in_=wtb[:, 0:2048]),
                 reads=[buf("wtb")], writes=[buf("wout_s")], dma=True)

        xfblock = arb16(0, 32 * 4 * 128)
        Sf = arf32(32768, 32 * 128)
        Sb = arb16(49152, 32 * 128)
        kqbuf = arb16(57344, 2 * 256)
        vzbuf = arb16(58368, 2 * 256)
        HT0 = 65536
        _roles = {}
        _ro = 0
        for _r in ["L", "U", "P0", "P1", "P2", "P3", "P4", "P5", "W", "E", "tD",
                   "aT", "kgl", "r0", "vf", "tm", "u1", "Xv2"]:
            _roles[_r] = (_ro, True); _ro += 512
        for _r in ["Xv", "zs", "xf", "vb"]:
            _roles[_r] = (_ro, False); _ro += 256
        HSLOT = _ro
        def hrole(slotid, name):
            off, isf = _roles[name]
            off = HT0 + slotid * HSLOT + off
            if isf:
                return AR[:, off // 2: off // 2 + 256].bitcast(F32)
            return AR[:, off // 2: off // 2 + 128]
        P3A = HT0
        attnb = arb16(P3A, 16 * 512)
        sqst = arb16(P3A + 16384, 512)
        x1st = arf32(P3A + 16384 + 1024, 512)
        wostage0 = arb16(P3A + 16384 + 3072, 1024)
        wostage1 = arb16(P3A + 16384 + 5120, 1024)

        p.op("vector", lambda q: q.memset(Sf[:, :], 0.0), reads=[buf("spill")], writes=[buf("Sf")])
        p.op("vector", lambda q: q.memset(Sb[:, :], 0.0), reads=[buf("spill")], writes=[buf("Sb")])

        r_rstd2 = rows[0:1, S:2 * S]
        r_negm2 = rows[0:1, 2 * S:3 * S]
        r_cffn = rows[0:1, 3 * S + 1536:3 * S + 2048]

        for u in range(NU):
            T0 = u * UT
            ub = u % 4
            blk = u // 4
            p.op("vector", lambda q, T0=T0: q.tensor_copy(stk[0:32, 0:128], g_gc[:, T0:T0 + UT]),
                 reads=[buf("grow")], writes=[buf("stk")])
            p.op("vector", lambda q, T0=T0: q.tensor_copy(stk[32:64, 0:128], g_gbl[:, T0:T0 + UT]),
                 reads=[buf("grow")], writes=[buf("stk")])
            p.op("vector", lambda q, T0=T0: q.tensor_scalar(out=stk[64:96, 0:128], in0=g_gc[:, T0:T0 + UT],
                            scalar1=g_gc[:, T0 + UT - 1:T0 + UT], scalar2=None, op0=ALU.subtract),
                 reads=[buf("grow")], writes=[buf("stk")])
            p.op("vector", lambda q: q.memset(stk[96:128, 0:128], 0.0),
                 reads=[], writes=[buf("stk")])
            p.op("tensor", lambda q: q.matmul(PB[1][:, 0:128], stk[:, 0:128], cIF[:], start=True, stop=True),
                 reads=[buf("stk"), buf("cif")], writes=[pb[1]])
            p.op("scalar", lambda q: q.activation(gblc[:], PB[1][:, 32:64], AF.Copy),
                 reads=[pb[1]], writes=[buf("gblc")])
            p.op("scalar", lambda q: q.activation(neggc[:], PB[1][:, 0:32], AF.Copy, scale=cb(-1.0)),
                 reads=[pb[1]], writes=[buf("neggc")])
            p.op("vector", lambda q: q.tensor_scalar(out=neggcq[:], in0=neggc[:], scalar1=-LNSQ,
                                                     scalar2=None, op0=ALU.add),
                 reads=[buf("neggc")], writes=[buf("neggcq")])
            p.op("scalar", lambda q: q.activation(egcq[:], PB[1][:, 0:32], AF.Exp, bias=cb(-LNSQ)),
                 reads=[pb[1]], writes=[buf("egcq")])
            p.op("scalar", lambda q: q.activation(egbl_n[:], PB[1][:, 32:64], AF.Exp),
                 reads=[pb[1]], writes=[buf("egbl_n")])
            p.op("scalar", lambda q: q.activation(eglmg[:], PB[1][:, 64:96], AF.Exp, scale=cb(-1.0)),
                 reads=[pb[1]], writes=[buf("eglmg")])
            p.op("scalar", lambda q: q.activation(eglc[:], PB[1][:, 64:96], AF.Copy),
                 reads=[pb[1]], writes=[buf("eglc")])
            p.op("vector", lambda q: q.tensor_tensor(out=eglc[:], in0=eglc[:],
                                                     in1=neggc[:], op=ALU.add),
                 reads=[buf("eglc"), buf("neggc")], writes=[buf("eglc")])
            p.op("scalar", lambda q: q.activation(eglc[:], eglc[:], AF.Exp, scale=cb(-1.0)),
                 reads=[buf("eglc")], writes=[buf("eglc")])
            p.op("vector", lambda q: q.tensor_tensor(out=betac[:], in0=gblc[:],
                                                     in1=neggc[:], op=ALU.add),
                 reads=[buf("gblc"), buf("neggc")], writes=[buf("betac")])
            p.op("scalar", lambda q: q.activation(betac[:], betac[:], AF.Exp),
                 reads=[buf("betac")], writes=[buf("betac")])
            p.op("vector", lambda q: q.tensor_scalar(out=egbl_n[:], in0=egbl_n[:], scalar1=-1.0,
                                                     scalar2=None, op0=ALU.mult),
                 reads=[buf("egbl_n")], writes=[buf("egbl_n")])

            for vh in range(NVH):
                kh = vh // 2
                hs = vh % 2
                ks = kh % 2
                kq = kqbuf[:, ks * 256:ks * 256 + 256]
                bkq = buf("kq%d" % ks)
                if vh % 2 == 0:
                    p.op("gpsimd", lambda q, kh=kh, T0=T0, kq=kq: q.dma_start(
                            out=kq[:, 0:128], in_=spill[2048 + kh * 128:2048 + (kh + 1) * 128, T0:T0 + UT]),
                         reads=[buf("spill"), buf("w1t"), buf("w2t"), buf("wout_s")],
                         writes=[bkq, buf("attnb"), buf("x1st"), buf("sqst"), buf("wo0"), buf("wo1")],
                         dma=True)
                    p.op("gpsimd", lambda q, kh=kh, T0=T0, kq=kq: q.dma_start(
                            out=kq[:, 128:256], in_=spill[kh * 128:(kh + 1) * 128, T0:T0 + UT]),
                         reads=[buf("spill")], writes=[bkq], dma=True)
                vz = vzbuf[:, hs * 256:hs * 256 + 256]
                bvz = buf("vz%d" % hs)
                p.op("gpsimd", lambda q, vh=vh, T0=T0, vz=vz: q.dma_start(
                        out=vz[:, 0:128], in_=spill[4096 + vh * 128:4096 + (vh + 1) * 128, T0:T0 + UT]),
                     reads=[buf("spill")],
                     writes=[bvz, buf("attnb"), buf("x1st"), buf("sqst"), buf("wo0"), buf("wo1")],
                     dma=True)
                p.op("gpsimd", lambda q, vh=vh, T0=T0, vz=vz: q.dma_start(
                        out=vz[:, 128:256], in_=spill[8192 + vh * 128:8192 + (vh + 1) * 128, T0:T0 + UT]),
                     reads=[buf("spill")], writes=[bvz], dma=True)
                kF = kq[:, 0:128]
                qF = kq[:, 128:256]
                vF = vz[:, 0:128]
                zF = vz[:, 128:256]

                tL = hrole(hs, "L"); tU = hrole(hs, "U")
                tP = [hrole(hs, "P%d" % i_) for i_ in range(6)]
                tW = hrole(hs, "W"); taT = hrole(hs, "aT"); tXv = hrole(hs, "Xv2")
                tkgl = hrole(hs, "kgl")
                tzs = hrole(hs, "zs"); tr0 = hrole(hs, "r0"); txf = hrole(hs, "xf")
                tm = hrole(hs, "tm")
                u1 = hrole(hs, "u1")
                tD = hrole(hs, "tD")
                tE = hrole(hs, "E")
                bh = buf("ht%d" % hs)

                p.op("tensor", lambda q, kF=kF: q.matmul(PB[2][:, 0:128], kF, cI, start=True, stop=True),
                     reads=[bkq, buf("consts")], writes=[pb[2]])
                p.op("tensor", lambda q, vF=vF: q.matmul(PB[2][:, 128:256], vF, cI, start=True, stop=True),
                     reads=[bvz, buf("consts")], writes=[pb[2]])
                p.op("tensor", lambda q, zF=zF: q.matmul(PB[2][:, 256:384], zF, cI, start=True, stop=True),
                     reads=[bvz, buf("consts")], writes=[pb[2]])
                p.op("scalar", lambda q, tXv=tXv, vh=vh: q.activation(
                        tXv, PB[2][:, 128:256], AF.Copy, scale=betac[:, vh:vh + 1]),
                     reads=[pb[2], buf("betac")], writes=[bh])
                p.op("scalar", lambda q, tkgl=tkgl, vh=vh: q.activation(
                        tkgl, PB[2][:, 0:128], AF.Copy, scale=eglmg[:, vh:vh + 1]),
                     reads=[pb[2], buf("eglmg")], writes=[bh])
                p.op("scalar", lambda q, tzs=tzs: q.activation(tzs, PB[2][:, 256:384], AF.Silu),
                     reads=[pb[2]], writes=[bh])

                p.op("tensor", lambda q, kF=kF, kq=kq: q.matmul(PB[0][:, 0:256], kF, kq[:, 0:256],
                                                                start=True, stop=True),
                     reads=[bkq], writes=[pb[0]])

                p.op("sync", lambda q, vh=vh, T0=T0: q.dma_start(
                        out=rstg[0:1, 0:256],
                        in_=grow[vh:vh + 1, :].rearrange("o (a s) -> o a s", a=3)[:, 0:2, T0:T0 + UT]),
                     reads=[buf("grow")], writes=[buf("rstg")], dma=True)
                def dec_build(qcol0, rsl, mask, bias_ap, scale, bias_buf, vh=vh, bh=bh, tD=tD):
                    p.op("tensor", lambda q, rsl=rsl, qcol0=qcol0: q.matmul(
                            PB[1][:, qcol0:qcol0 + 128], cOnesRowF[:],
                            rstg[0:1, rsl * 128:(rsl + 1) * 128], start=True, stop=False),
                         reads=[buf("rstg"), buf("onesrowf")], writes=[pb[1]])
                    p.op("tensor", lambda q, mask=mask, qcol0=qcol0: q.matmul(
                            PB[1][:, qcol0:qcol0 + 128], cIF[:], mask, start=False, stop=True),
                         reads=[buf("cmask"), buf("cif")], writes=[pb[1]])
                    p.op("scalar", lambda q, bias_ap=bias_ap, scale=scale, qcol0=qcol0:
                            q.activation(tD, PB[1][:, qcol0:qcol0 + 128], AF.Exp,
                                         bias=bias_ap, scale=scale),
                         reads=[pb[1], buf(bias_buf)], writes=[bh])
                dec_build(0, 0, cML, gblc[:, vh:vh + 1], -1.0, "gblc")
                p.op("vector", lambda q, tL=tL, tD=tD: q.scalar_tensor_tensor(
                        out=tL, in0=PB[0][:, 0:128], scalar=-1.0, in1=tD,
                        op0=ALU.mult, op1=ALU.mult),
                     reads=[pb[0], bh], writes=[bh])
                dec_build(128, 1, cMU, neggc[:, vh:vh + 1], 1.0, "neggc")
                p.op("vector", lambda q, tU=tU, tD=tD: q.scalar_tensor_tensor(
                        out=tU, in0=PB[0][:, 0:128], scalar=-1.0, in1=tD,
                        op0=ALU.mult, op1=ALU.mult),
                     reads=[pb[0], bh], writes=[bh])
                dec_build(256, 0, cMI, neggcq[:, vh:vh + 1], 1.0, "neggcq")
                p.op("vector", lambda q, taT=taT, tD=tD: q.scalar_tensor_tensor(
                        out=taT, in0=PB[0][:, 128:256], scalar=-1.0, in1=tD,
                        op0=ALU.mult, op1=ALU.mult),
                     reads=[pb[0], bh], writes=[bh])

                srcL, srcU = tL, tU
                for lev in range(3):
                    dL, dU = tP[2 * lev], tP[2 * lev + 1]
                    qc = (lev % 2) * 256
                    p.op("tensor", lambda q, srcL=srcL, srcU=srcU, qc=qc: q.matmul(
                            PB[3][:, qc:qc + 128], srcU, srcL, start=True, stop=True),
                         reads=[bh], writes=[pb[3]])
                    p.op("tensor", lambda q, srcL=srcL, srcU=srcU, qc=qc: q.matmul(
                            PB[3][:, qc + 128:qc + 256], srcL, srcU, start=True, stop=True),
                         reads=[bh], writes=[pb[3]])
                    p.op("vector", lambda q, dL=dL, qc=qc: q.tensor_copy(dL, PB[3][:, qc:qc + 128]),
                         reads=[pb[3]], writes=[bh])
                    p.op("vector", lambda q, dU=dU, qc=qc: q.tensor_copy(dU, PB[3][:, qc + 128:qc + 256]),
                         reads=[pb[3]], writes=[bh])
                    srcL, srcU = dL, dU
                p.op("vector", lambda q, tW=tW, tU=tU: q.tensor_tensor(out=tW, in0=tU, in1=cIF[:], op=ALU.add),
                     reads=[bh, buf("cif")], writes=[bh])
                for lev in range(3):
                    Lp = tP[2 * lev]
                    qc = (lev % 2) * 256 + 128
                    p.op("vector", lambda q, tE=tE, Lp=Lp: q.tensor_tensor(out=tE, in0=Lp, in1=cIF[:], op=ALU.add),
                         reads=[bh, buf("cif")], writes=[bh])
                    p.op("tensor", lambda q, tE=tE, tW=tW, qc=qc: q.matmul(
                            PB[3][:, qc:qc + 128], tE, tW, start=True, stop=True),
                         reads=[bh], writes=[pb[3]])
                    p.op("vector", lambda q, tW=tW, qc=qc: q.tensor_copy(tW, PB[3][:, qc:qc + 128]),
                         reads=[pb[3]], writes=[bh])
                L8, U8 = tP[4], tP[5]
                tL16, tU16 = tP[0], tP[1]
                p.op("tensor", lambda q, U8=U8, L8=L8: q.matmul(PB[3][:, 0:128], U8, L8, start=True, stop=True),
                     reads=[bh], writes=[pb[3]])
                p.op("tensor", lambda q, U8=U8, L8=L8: q.matmul(PB[3][:, 128:256], L8, U8, start=True, stop=True),
                     reads=[bh], writes=[pb[3]])
                p.op("vector", lambda q, tL16=tL16: q.tensor_copy(tL16, PB[3][:, 0:128]),
                     reads=[pb[3]], writes=[bh])
                p.op("vector", lambda q, tU16=tU16: q.tensor_copy(tU16, PB[3][:, 128:256]),
                     reads=[pb[3]], writes=[bh])
                tL32, tU32 = tP[2], tP[3]
                p.op("tensor", lambda q, tU16=tU16, tL16=tL16: q.matmul(
                        PB[3][:, 256:384], tU16, tL16, start=True, stop=True),
                     reads=[bh], writes=[pb[3]])
                p.op("tensor", lambda q, tU16=tU16, tL16=tL16: q.matmul(
                        PB[3][:, 384:512], tL16, tU16, start=True, stop=True),
                     reads=[bh], writes=[pb[3]])
                p.op("vector", lambda q, tL32=tL32: q.tensor_copy(tL32, PB[3][:, 256:384]),
                     reads=[pb[3]], writes=[bh])
                p.op("vector", lambda q, tU32=tU32: q.tensor_copy(tU32, PB[3][:, 384:512]),
                     reads=[pb[3]], writes=[bh])
                tU64 = tP[4]
                p.op("tensor", lambda q, tU32=tU32, tL32=tL32: q.matmul(
                        PB[3][:, 0:128], tL32, tU32, start=True, stop=True),
                     reads=[bh], writes=[pb[3]])
                p.op("vector", lambda q, tU64=tU64: q.tensor_copy(tU64, PB[3][:, 0:128]),
                     reads=[pb[3]], writes=[bh])

                p.op("tensor", lambda q, kF=kF, vh=vh: q.matmul(
                        PB[6][:, 0:128], kF, Sb[:, vh * 128:(vh + 1) * 128], start=True, stop=True),
                     reads=[bkq, buf("Sb")], writes=[pb[6]])
                p.op("vector", lambda q, tr0=tr0, tXv=tXv, vh=vh: q.scalar_tensor_tensor(
                        out=tr0, in0=PB[6][:, 0:128], scalar=egbl_n[:, vh:vh + 1], in1=tXv,
                        op0=ALU.mult, op1=ALU.add),
                     reads=[pb[6], bh, buf("egbl_n")], writes=[bh])
                tvf = hrole(hs, "vf")
                p.op("tensor", lambda q, tW=tW, tr0=tr0: q.matmul(PB[4][:, 0:128], tW, tr0,
                                                                  start=True, stop=True),
                     reads=[bh], writes=[pb[4]])
                p.op("vector", lambda q, tvf=tvf: q.tensor_copy(tvf, PB[4][:, 0:128]),
                     reads=[pb[4]], writes=[bh])
                for (Upow, qc) in [(tU16, 128), (tU32, 256), (tU64, 384)]:
                    p.op("tensor", lambda q, Upow=Upow, tvf=tvf, qc=qc: q.matmul(
                            PB[4][:, qc:qc + 128], cIF[:], tvf, start=True, stop=False),
                         reads=[bh, buf("cif")], writes=[pb[4]])
                    p.op("tensor", lambda q, Upow=Upow, tvf=tvf, qc=qc: q.matmul(
                            PB[4][:, qc:qc + 128], Upow, tvf, start=False, stop=True),
                         reads=[bh], writes=[pb[4]])
                    p.op("vector", lambda q, tvf=tvf, qc=qc: q.tensor_copy(tvf, PB[4][:, qc:qc + 128]),
                         reads=[pb[4]], writes=[bh])

                p.op("tensor", lambda q, taT=taT, tvf=tvf: q.matmul(
                        PB[5][:, 0:128], taT, tvf, start=True, stop=True),
                     reads=[bh], writes=[pb[5]])
                p.op("tensor", lambda q, qF=qF, vh=vh: q.matmul(
                        PB[6][:, 128:256], qF, Sb[:, vh * 128:(vh + 1) * 128], start=True, stop=True),
                     reads=[bkq, buf("Sb")], writes=[pb[6]])
                p.op("scalar", lambda q, u1=u1, vh=vh: q.activation(
                        u1, PB[6][:, 128:256], AF.Copy, scale=egcq[:, vh:vh + 1]),
                     reads=[pb[6], buf("egcq")], writes=[bh])
                p.op("vector", lambda q, tm=tm, u1=u1: q.tensor_tensor(
                        out=tm, in0=u1, in1=PB[5][:, 0:128], op=ALU.subtract),
                     reads=[pb[5], bh], writes=[bh])
                p.op("vector", lambda q, tm=tm, tzs=tzs: q.tensor_tensor(out=tm, in0=tm, in1=tzs, op=ALU.mult),
                     reads=[bh], writes=[bh])
                p.op("scalar", lambda q, u1=u1, tm=tm: q.activation(u1, tm, AF.Square, scale=cb(0.0883883476)),
                     reads=[bh], writes=[bh])
                p.op("vector", lambda q, u1=u1: q.tensor_reduce(out=scol[:, 0:1], in_=u1,
                                                                axis=AX.X, op=ALU.add),
                     reads=[bh], writes=[buf("scol")])
                p.op("scalar", lambda q: q.activation(scol[:, 1:2], scol[:, 0:1], AF.Sqrt, bias=cb(GEPS)),
                     reads=[buf("scol")], writes=[buf("scol")])
                p.op("vector", lambda q: q.reciprocal(scol[:, 2:3], scol[:, 1:2]),
                     reads=[buf("scol")], writes=[buf("scol")])
                p.op("vector", lambda q, tm=tm: q.tensor_scalar(out=tm, in0=tm, scalar1=scol[:, 2:3],
                                                                scalar2=None, op0=ALU.mult),
                     reads=[bh, buf("scol")], writes=[bh])
                p.op("vector", lambda q, txf=txf, tm=tm: q.tensor_tensor(out=txf, in0=tm, in1=cNW[:, 0:128],
                                                                         op=ALU.mult),
                     reads=[bh, buf("cnw")], writes=[bh])
                p.op("tensor", lambda q, txf=txf: q.matmul(PB[7][:, 0:128], txf, cI, start=True, stop=True),
                     reads=[bh, buf("consts")], writes=[pb[7]])
                p.op("vector", lambda q, vh=vh, ub=ub: q.tensor_copy(
                        xfblock[:, vh * 512 + ub * 128: vh * 512 + (ub + 1) * 128], PB[7][:, 0:128]),
                     reads=[pb[7]], writes=[buf("xfblock")])
                p.op("tensor", lambda q, tkgl=tkgl, tvf=tvf: q.matmul(
                        PB[6][:, 256:384], tkgl, tvf, start=True, stop=True),
                     reads=[bh], writes=[pb[6]])
                p.op("vector", lambda q, vh=vh: q.scalar_tensor_tensor(
                        out=Sf[:, vh * 128:(vh + 1) * 128], in0=Sf[:, vh * 128:(vh + 1) * 128],
                        scalar=eglc[:, vh:vh + 1], in1=PB[6][:, 256:384], op0=ALU.mult, op1=ALU.add),
                     reads=[pb[6], buf("eglc"), buf("Sf")], writes=[buf("Sf")])
                p.op("vector", lambda q, vh=vh: q.tensor_copy(Sb[:, vh * 128:(vh + 1) * 128],
                                                              Sf[:, vh * 128:(vh + 1) * 128]),
                     reads=[buf("Sf")], writes=[buf("Sb")])

            if ub == 3:
                B0 = blk * 512
                for dhalf in range(2):
                    for k in range(32):
                        wo = wostage0 if k % 2 == 0 else wostage1
                        bwo = buf("wo%d" % (k % 2))
                        p.op("gpsimd", lambda q, k=k, dhalf=dhalf, wo=wo: q.dma_start(
                                out=wo[:, 0:1024],
                                in_=wout_s[k * 128:(k + 1) * 128, dhalf * 1024:(dhalf + 1) * 1024]),
                             writes=[bwo, buf("ht0"), buf("ht1")], dma=True)
                        for dt in range(8):
                            p.op("tensor", lambda q, k=k, dt=dt, wo=wo: q.matmul(
                                    PB[dt][:, 0:512], wo[:, dt * 128:(dt + 1) * 128],
                                    xfblock[:, k * 512:(k + 1) * 512],
                                    start=(k == 0), stop=(k == 31)),
                                 reads=[bwo, buf("xfblock")], writes=[pb[dt]])
                    for dt in range(8):
                        p.op("scalar", lambda q, dt=dt, dhalf=dhalf: q.activation(
                                attnb[:, (dhalf * 8 + dt) * 512:(dhalf * 8 + dt + 1) * 512],
                                PB[dt][:, 0:512], AF.Copy),
                             reads=[pb[dt]], writes=[buf("attnb"), buf("ht0"), buf("ht1")])
                for vh_ in range(32):
                    p.op("sync", lambda q, vh_=vh_, B0=B0: q.dma_start(
                            out=dbg_xf[vh_ * 128:(vh_ + 1) * 128, B0:B0 + 512],
                            in_=xfblock[:, vh_ * 512:(vh_ + 1) * 512]),
                         reads=[buf("xfblock")], writes=[buf("dbg_xf")], dma=True)
                for dt_ in range(16):
                    p.op("sync", lambda q, dt_=dt_, B0=B0: q.dma_start(
                            out=dbg_at[dt_ * 128:(dt_ + 1) * 128, B0:B0 + 512],
                            in_=attnb[:, dt_ * 512:(dt_ + 1) * 512]),
                         reads=[buf("attnb")], writes=[buf("dbg_at")], dma=True)
                for dt in range(16):
                    p.op("scalar", lambda q, dt=dt: q.activation(
                            sqst, attnb[:, dt * 512:(dt + 1) * 512], AF.Square),
                         reads=[buf("attnb")], writes=[buf("sqst")])
                    p.op("tensor", lambda q, dt=dt: q.matmul(
                            PB[0][0:1, 0:512], cOnesColB[:], attnb[:, dt * 512:(dt + 1) * 512],
                            start=(dt == 0), stop=(dt == 15)),
                         reads=[buf("attnb"), buf("onescolb")], writes=[pb[0]])
                    p.op("tensor", lambda q, dt=dt: q.matmul(
                            PB[1][0:1, 0:512], cOnesColB[:], sqst,
                            start=(dt == 0), stop=(dt == 15)),
                         reads=[buf("sqst"), buf("onescolb")], writes=[pb[1]])
                ra_rstd = rows[0:1, 3 * S + 1536:3 * S + 2048]
                ra_negmr = rows[0:1, 3 * S + 2048:3 * S + 2560]
                rowstats([(PB[0][0:1, 0:512], pb[0])], [(PB[1][0:1, 0:512], pb[1])],
                         512, ra_rstd, ra_negmr, None, LNEPS, D)
                bcast_row(RXB[:, 0:512], ra_rstd, "RXB", 512)
                bcast_row(RXB[:, 512:1024], ra_negmr, "RXB2", 512)
                for dt in range(16):
                    p.op("sync", lambda q, dt=dt, B0=B0: q.dma_start(
                            out=x1st, in_=xT[dt * 128:(dt + 1) * 128, B0:B0 + 512]),
                         writes=[buf("x1st"), buf("ht0"), buf("ht1")], dma=True)
                    p.op("vector", lambda q, dt=dt: q.tensor_tensor(
                            out=x1st, in0=x1st, in1=RXB[:, 512:1024], op=ALU.add),
                         reads=[buf("x1st"), buf("RXB2")], writes=[buf("x1st")])
                    def comb(q, dt=dt):
                        return q.scalar_tensor_tensor(
                            out=x1st, in0=attnb[:, dt * 512:(dt + 1) * 512], scalar=1.0,
                            in1=x1st, op0=ALU.mult, op1=ALU.mult)
                    p.op("vector", lambda q, dt=dt: q.tensor_tensor(
                            out=sqst, in0=attnb[:, dt * 512:(dt + 1) * 512],
                            in1=RXB[:, 0:512], op=ALU.mult),
                         reads=[buf("attnb"), buf("RXB")], writes=[buf("sqst")])
                    p.op("vector", lambda q, dt=dt: q.tensor_tensor(
                            out=x1st, in0=x1st, in1=sqst, op=ALU.add),
                         reads=[buf("sqst"), buf("x1st")], writes=[buf("x1st")])
                    p.op("sync", lambda q, dt=dt, B0=B0: q.dma_start(
                            out=x1f_s[dt * 128:(dt + 1) * 128, B0:B0 + 512], in_=x1st),
                         reads=[buf("x1st")], writes=[buf("x1f_s")], dma=True)
                    p.op("vector", lambda q: q.tensor_copy(sqst, x1st),
                         reads=[buf("x1st")], writes=[buf("sqst")])
                    p.op("tensor", lambda q, dt=dt: q.matmul(
                            PB[2][0:1, 0:512], cOnesColB[:], sqst, start=(dt == 0), stop=(dt == 15)),
                         reads=[buf("sqst"), buf("onescolb")], writes=[pb[2]])
                    p.op("scalar", lambda q, dt=dt: q.activation(
                            attnb[:, dt * 512:(dt + 1) * 512], sqst, AF.Square),
                         reads=[buf("sqst")], writes=[buf("attnb")])
                    p.op("tensor", lambda q, dt=dt: q.matmul(
                            PB[3][0:1, 0:512], cOnesColB[:], attnb[:, dt * 512:(dt + 1) * 512],
                            start=(dt == 0), stop=(dt == 15)),
                         reads=[buf("attnb"), buf("onescolb")], writes=[pb[3]])
                rowstats([(PB[2][0:1, 0:512], pb[2])], [(PB[3][0:1, 0:512], pb[3])],
                         512, rows[0:1, S + B0:S + B0 + 512],
                         rows[0:1, 2 * S + B0:2 * S + B0 + 512],
                         None, LNEPS, D, rstd_mult=False)
                bcast_row(RXB[:, 512:1024], rows[0:1, 2 * S + B0:2 * S + B0 + 512], "RXBm2", 512)
                for dt in range(16):
                    p.op("sync", lambda q, dt=dt, B0=B0: q.dma_start(
                            out=x1st, in_=x1f_s[dt * 128:(dt + 1) * 128, B0:B0 + 512]),
                         reads=[buf("x1f_s")], writes=[buf("x1st")], dma=True)
                    p.op("vector", lambda q: q.tensor_tensor(
                            out=sqst, in0=x1st, in1=RXB[:, 512:1024], op=ALU.add),
                         reads=[buf("x1st"), buf("RXBm2")], writes=[buf("sqst")])
                    p.op("sync", lambda q, dt=dt, B0=B0: q.dma_start(
                            out=x1b_s[dt * 128:(dt + 1) * 128, B0:B0 + 512], in_=sqst),
                         reads=[buf("sqst")], writes=[buf("x1b_s")], dma=True)


        x1stage = arb16(0, 16 * 512)
        Har = arb16(16384, 64 * 512)
        w1st0 = arb16(81920, 16 * 128)
        w1st1 = arb16(86016, 16 * 128)
        w2st0 = arb16(81920, 64 * 128)
        w2st1 = w2st0
        ffnb = arb16(0, 16 * 512)
        fsq = arb16(98304, 512)
        fx = arf32(16384, 512)

        p.op("vector", lambda q: q.engine_nop(), writes=[buf("attnb"), buf("xfblock"),
             buf("x1st"), buf("sqst"), buf("wo0"), buf("wo1"), buf("ht0"), buf("ht1")])
        for blk in range(4):
            B0 = blk * 512
            for dt in range(16):
                p.op("sync", lambda q, dt=dt, B0=B0: q.dma_start(
                        out=x1stage[:, dt * 512:(dt + 1) * 512],
                        in_=x1b_s[dt * 128:(dt + 1) * 128, B0:B0 + 512]),
                     reads=[buf("xfblock"), buf("x1b_s")], writes=[buf("x1stage")], dma=True)
            for ht in range(64):
                wst = w1st0 if ht % 2 == 0 else w1st1
                bws = buf("w1s%d" % (ht % 2))
                p.op("gpsimd", lambda q, ht=ht, wst=wst: q.dma_start(
                        out=wst[:, 0:2048].rearrange("p (a c) -> p a c", c=128),
                        in_=w1t_s.rearrange("(a pp) h -> pp a h", pp=128)[:, :, ht * 128:(ht + 1) * 128]),
                     reads=[buf("attnb"), buf("w1t")], writes=[bws], dma=True)
                pbq = ht % 2
                for kt in range(16):
                    p.op("tensor", lambda q, kt=kt, ht=ht, wst=wst, pbq=pbq: q.matmul(
                            PB[pbq][:, 0:512], wst[:, kt * 128:(kt + 1) * 128],
                            x1stage[:, kt * 512:(kt + 1) * 512],
                            start=(kt == 0), stop=(kt == 15)),
                         reads=[bws, buf("x1stage")], writes=[pb[pbq]])
                p.op("scalar", lambda q, ht=ht, pbq=pbq: q.activation(
                        fsq, PB[pbq][:, 0:512], AF.Relu),
                     reads=[pb[pbq]], writes=[buf("fsq")])
                p.op("vector", lambda q, ht=ht: q.tensor_tensor(
                        out=Har[:, ht * 512:(ht + 1) * 512], in0=fsq, in1=fsq, op=ALU.mult),
                     reads=[buf("fsq")], writes=[buf("Har")])
                p.op("vector", lambda q, ht=ht: q.tensor_tensor(
                        out=fsq, in0=Har[:, ht * 512:(ht + 1) * 512],
                        in1=Har[:, ht * 512:(ht + 1) * 512], op=ALU.mult),
                     reads=[buf("Har")], writes=[buf("fsq")])
                p.op("tensor", lambda q, ht=ht: q.matmul(
                        PB[4][0:1, 0:512], cOnesColB[:], fsq, start=(ht == 0), stop=(ht == 63)),
                     reads=[buf("fsq"), buf("onescolb")], writes=[pb[4]])
            rr = rows[0:1, S + B0:S + B0 + 512]
            sc0 = rows[0:1, 3 * S:3 * S + 512]
            sc1 = rows[0:1, 3 * S + 512:3 * S + 1024]
            sc2 = rows[0:1, 3 * S + 1024:3 * S + 1536]
            p.op("vector", lambda q: q.tensor_scalar(out=sc0, in0=PB[4][0:1, 0:512], scalar1=1.0 / HID, scalar2=None, op0=ALU.mult),
                 reads=[pb[4], buf("rows")], writes=[buf("rows")])
            p.op("vector", lambda q, rr=rr: q.tensor_tensor(out=sc1, in0=rr, in1=rr, op=ALU.mult),
                 reads=[buf("rows")], writes=[buf("rows")])
            p.op("vector", lambda q: q.tensor_tensor(out=sc2, in0=sc1, in1=sc1, op=ALU.mult),
                 reads=[buf("rows")], writes=[buf("rows")])
            p.op("vector", lambda q: q.tensor_tensor(out=sc0, in0=sc0, in1=sc2, op=ALU.mult),
                 reads=[buf("rows")], writes=[buf("rows")])
            p.op("scalar", lambda q: q.activation(sc0, sc0, AF.Sqrt, bias=cb1(RMSEPS)),
                 reads=[buf("rows")], writes=[buf("rows")])
            p.op("vector", lambda q: q.reciprocal(sc2, sc0),
                 reads=[buf("rows")], writes=[buf("rows")])
            p.op("vector", lambda q: q.tensor_tensor(out=r_cffn, in0=sc2, in1=sc1, op=ALU.mult),
                 reads=[buf("rows")], writes=[buf("rows")])
            bcast_row(RXB[:, 0:512], r_cffn, "RXBc", 512)
            for dt in range(16):
                wst = w2st0
                bws = buf("w2s0")
                for a0 in range(0, 64, 16):
                    p.op("gpsimd", lambda q, dt=dt, a0=a0, wst=wst: q.dma_start(
                            out=wst[:, a0 * 128:(a0 + 16) * 128].rearrange("p (a c) -> p a c", c=128),
                            in_=w2t_s.rearrange("(a pp) d -> pp a d", pp=128)[:, a0:a0 + 16, dt * 128:(dt + 1) * 128]),
                         reads=[buf("attnb"), buf("w2t")], writes=[bws, buf("w1s0"), buf("w1s1")], dma=True)
                pbq = 2 + dt % 2
                for h in range(64):
                    p.op("tensor", lambda q, h=h, dt=dt, wst=wst, pbq=pbq: q.matmul(
                            PB[pbq][:, 0:512], wst[:, h * 128:(h + 1) * 128],
                            Har[:, h * 512:(h + 1) * 512],
                            start=(h == 0), stop=(h == 63)),
                         reads=[bws, buf("Har")], writes=[pb[pbq]])
                p.op("vector", lambda q, dt=dt, pbq=pbq: q.tensor_tensor(
                        out=ffnb[:, dt * 512:(dt + 1) * 512], in0=PB[pbq][:, 0:512],
                        in1=RXB[:, 0:512], op=ALU.mult),
                     reads=[pb[pbq], buf("RXBc")], writes=[buf("x1stage")])
                p.op("tensor", lambda q, dt=dt: q.matmul(
                        PB[5][0:1, 0:512], cOnesColB[:], ffnb[:, dt * 512:(dt + 1) * 512],
                        start=(dt == 0), stop=(dt == 15)),
                     reads=[buf("x1stage"), buf("onescolb")], writes=[pb[5]])
                p.op("scalar", lambda q, dt=dt: q.activation(fsq, ffnb[:, dt * 512:(dt + 1) * 512], AF.Square),
                     reads=[buf("x1stage")], writes=[buf("fsq")])
                p.op("tensor", lambda q, dt=dt: q.matmul(
                        PB[6][0:1, 0:512], cOnesColB[:], fsq, start=(dt == 0), stop=(dt == 15)),
                     reads=[buf("fsq"), buf("onescolb")], writes=[pb[6]])
            r3_rstd = rows[0:1, 3 * S + 1536:3 * S + 2048]
            r3_negmr = rows[0:1, 3 * S + 2048:3 * S + 2560]
            rowstats([(PB[5][0:1, 0:512], pb[5])], [(PB[6][0:1, 0:512], pb[6])],
                     512, r3_rstd, r3_negmr, None, LNEPS, D)
            bcast_row(RXB[:, 0:512], r3_rstd, "RXB3", 512)
            bcast_row(RXB[:, 512:1024], r3_negmr, "RXB4", 512)
            for dt in range(16):
                p.op("sync", lambda q, dt=dt, B0=B0: q.dma_start(
                        out=fx, in_=x1f_s[dt * 128:(dt + 1) * 128, B0:B0 + 512]),
                     reads=[buf("w1s0"), buf("w1s1")], writes=[buf("fx"), buf("Har")], dma=True)
                p.op("vector", lambda q, dt=dt: q.tensor_tensor(
                        out=fx, in0=fx, in1=RXB[:, 512:1024], op=ALU.add),
                     reads=[buf("fx"), buf("RXB4")], writes=[buf("fx")])
                p.op("vector", lambda q, dt=dt: q.tensor_tensor(
                        out=fsq, in0=ffnb[:, dt * 512:(dt + 1) * 512], in1=RXB[:, 0:512], op=ALU.mult),
                     reads=[buf("x1stage"), buf("RXB3")], writes=[buf("fsq")])
                p.op("vector", lambda q, dt=dt: q.tensor_tensor(
                        out=fx, in0=fx, in1=fsq, op=ALU.add),
                     reads=[buf("fsq"), buf("fx")], writes=[buf("fx")])
                p.op("sync", lambda q, dt=dt, B0=B0: q.dma_start(
                        out=outT[dt * 128:(dt + 1) * 128, B0:B0 + 512], in_=fx),
                     reads=[buf("fx"), buf("Har")], writes=[buf("outT")], dma=True)

        p.op("vector", lambda q: q.engine_nop(), reads=[buf("outT")])

        with nc.Block() as block:
            p.replay(block)

    return nc


_NC_CACHE = None

def kernel(**inputs):
    global _NC_CACHE
    x = np.asarray(inputs["L0_attn_hidden_states"], dtype=np.float32)
    Wq = np.ascontiguousarray(np.asarray(inputs["L0_attn_W_qkvz"], dtype=np.float32))
    Wb = np.asarray(inputs["L0_attn_W_ba"], dtype=np.float32)
    ck = np.asarray(inputs["L0_attn_conv_kernel"], dtype=np.float32)
    Al = np.asarray(inputs["L0_attn_A_log"], dtype=np.float32)
    dtbv = np.asarray(inputs["L0_attn_dt_bias"], dtype=np.float32)
    nwv = np.asarray(inputs["L0_attn_norm_weight"], dtype=np.float32)
    Wo = np.ascontiguousarray(np.asarray(inputs["L0_attn_W_out"], dtype=np.float32))
    Wf = np.ascontiguousarray(np.asarray(inputs["L0_ffn_W_fc"], dtype=np.float32))
    Wp = np.ascontiguousarray(np.asarray(inputs["L0_ffn_W_proj"], dtype=np.float32))

    perm = [4 * kh + j for kh in range(16) for j in range(2)] + \
           [4 * kh + 2 + j for kh in range(16) for j in range(2)]
    Wb_p = np.ascontiguousarray(Wb[:, perm])
    ckT = np.ascontiguousarray(ck[:, 0, :].T)

    if _NC_CACHE is None:
        _NC_CACHE = build_program()
    nc = _NC_CACHE

    in_maps = []
    for b in range(8):
        in_maps.append({
            "xT": np.ascontiguousarray(x[b].T),
            "Wqkvz": Wq,
            "Wba": Wb_p,
            "convkT": ckT,
            "Alog": np.ascontiguousarray(Al.reshape(32, 1)),
            "dtbI": np.ascontiguousarray(dtbv.reshape(32, 1)),
            "nwIn": np.ascontiguousarray(nwv.reshape(1, DV)),
            "Wout": Wo,
            "Wfc": Wf,
            "Wproj": Wp,
        })
    res = run_bass_kernel_spmd(nc, in_maps, core_ids=list(range(8)))
    out = np.empty((8, S, D), dtype=np.float32)
    for b in range(8):
        out[b] = res.results[b]["outT"].T
    return out


# version 07
# speedup vs baseline: 1.2737x; 1.2737x over previous
import numpy as np
import ml_dtypes
import concourse.bass as bass
import concourse.mybir as mybir
from concourse.bass_utils import run_bass_kernel_spmd

F32 = mybir.dt.float32
BF16 = mybir.dt.bfloat16
AF = mybir.ActivationFunctionType
ALU = mybir.AluOpType
AX = mybir.AxisListType

S = 2048
D = 2048
NKH, NVH, DK, DV = 16, 32, 128, 128
KEY_DIM, VAL_DIM = NKH * DK, NVH * DV
FQKVZ = 2 * KEY_DIM + 2 * VAL_DIM
CONV_DIM = 2 * KEY_DIM + VAL_DIM
HID = 8192
NU = 16
UT = 128
LNEPS = 1e-5
GEPS = 1e-6
RMSEPS = float(np.finfo(np.float32).eps)
LNSQ = float(np.log(np.sqrt(128.0)))
BIGM = 30000.0
bf = ml_dtypes.bfloat16


class Buf:
    __slots__ = ("name", "writer", "readers")
    def __init__(self, name):
        self.name = name
        self.writer = None
        self.readers = []


class Prog:
    def __init__(self):
        self.engines = {}
        self.dstreams = {}
        self.mksem = None

    def add_engine(self, name, sem):
        self.engines[name] = {"ops": [], "count": 0, "hw": {}, "sem": sem, "name": name}

    def dstream(self, key):
        if key not in self.dstreams:
            self.dstreams[key] = {"sem": self.mksem("dma_" + key), "count": 0}
        return self.dstreams[key]

    def op(self, eng, fn, reads=(), writes=(), dma=False, dkey=None):
        e = self.engines[eng]
        waits = []
        def need(tok):
            if tok is None:
                return
            sem, val, src = tok
            if val is None:
                val = src["count"]
            elif src is e and e["name"] == "tensor":
                return
            hw = e["hw"].get(id(sem), 0)
            if val > hw:
                waits.append((sem, val))
                e["hw"][id(sem)] = val
        for b in reads:
            need(b.writer)
        for b in writes:
            need(b.writer)
            for tok in b.readers:
                need(tok)
        if dma:
            if dkey is None:
                dkey = writes[0].name if writes else reads[0].name
            st = self.dstream(dkey)
            st["count"] += 16
            tok = (st["sem"], None, st)
            inc_sem, inc = st["sem"], 16
        else:
            e["count"] += 1
            tok = (e["sem"], e["count"], e)
            inc_sem, inc = e["sem"], 1
        for b in writes:
            b.writer = tok
            b.readers = []
        for b in reads:
            b.readers.append(tok)
        e["ops"].append((fn, waits, inc, inc_sem))

    def replay(self, block):
        for name, e in self.engines.items():
            deco = getattr(block, name)
            ops = e["ops"]
            def make(ops=ops):
                def body(q):
                    for fn, waits, inc, inc_sem in ops:
                        for s, v in waits:
                            q.wait_ge(s, v)
                        inst = fn(q)
                        if inst is not None:
                            inst.then_inc(inc_sem, inc)
                return body
            deco(make())


def build_program():
    nc = bass.Bass("TRN2", target_bir_lowering=False)

    xT = nc.dram_tensor("xT", [D, S], F32, kind="ExternalInput").ap()
    Wqkvz = nc.dram_tensor("Wqkvz", [D, FQKVZ], F32, kind="ExternalInput").ap()
    Wba = nc.dram_tensor("Wba", [D, 64], F32, kind="ExternalInput").ap()
    convkT = nc.dram_tensor("convkT", [CONV_DIM, 4], F32, kind="ExternalInput").ap()
    Alog = nc.dram_tensor("Alog", [32, 1], F32, kind="ExternalInput").ap()
    dtbI = nc.dram_tensor("dtbI", [32, 1], F32, kind="ExternalInput").ap()
    nwIn = nc.dram_tensor("nwIn", [1, DV], F32, kind="ExternalInput").ap()
    Wout = nc.dram_tensor("Wout", [VAL_DIM, D], F32, kind="ExternalInput").ap()
    Wfc = nc.dram_tensor("Wfc", [HID, D], F32, kind="ExternalInput").ap()
    Wproj = nc.dram_tensor("Wproj", [D, HID], F32, kind="ExternalInput").ap()
    outT = nc.dram_tensor("outT", [D, S], F32, kind="ExternalOutput").ap()

    spill = nc.dram_tensor("spill", [FQKVZ, S], BF16, kind="Internal").ap()
    w1t_s = nc.dram_tensor("w1t_s", [D, HID], BF16, kind="Internal").ap()
    w2t_s = nc.dram_tensor("w2t_s", [HID, D], BF16, kind="Internal").ap()
    wout_s = nc.dram_tensor("wout_s", [VAL_DIM, D], BF16, kind="Internal").ap()
    x1f_s = nc.dram_tensor("x1f_s", [D, S], F32, kind="Internal").ap()
    dbg_xf = nc.dram_tensor("dbg_xf", [VAL_DIM, S], BF16, kind="Internal").ap()
    dbg_at = nc.dram_tensor("dbg_at", [D, S], BF16, kind="Internal").ap()
    x1b_s = nc.dram_tensor("x1b_s", [D, S], BF16, kind="Internal").ap()

    I128b_t = nc.inline_tensor(np.eye(128, dtype=bf), name="c_I128b")
    onesColB_t = nc.inline_tensor(np.ones((128, 1), dtype=bf), name="c_onescolb")
    onesRowF_t = nc.inline_tensor(np.ones((1, 128), dtype=np.float32), name="c_onesrowf")
    ti = np.arange(128)
    mL = np.where(ti[:, None] > ti[None, :], 0.0, BIGM).astype(bf)
    mU = np.where(ti[None, :] > ti[:, None], 0.0, -BIGM).astype(bf)
    mI = np.where(ti[None, :] >= ti[:, None], 0.0, -BIGM).astype(bf)
    M_L_t = nc.inline_tensor(mL.astype(np.float32), name="c_mL")
    M_U_t = nc.inline_tensor(mU.astype(np.float32), name="c_mU")
    M_I_t = nc.inline_tensor(mI.astype(np.float32), name="c_mI")

    from contextlib import ExitStack
    es = ExitStack()
    p = Prog()
    B = {}
    def buf(name):
        if name not in B:
            B[name] = Buf(name)
        return B[name]

    with es:
        sems = {}
        p.mksem = lambda nm: es.enter_context(nc.semaphore(nm))
        for n in ["sync", "tensor", "vector", "scalar", "gpsimd"]:
            sems[n] = es.enter_context(nc.semaphore("sem_" + n))
            p.add_engine(n, sems[n])

        def sb(name, shape, dtype):
            return es.enter_context(nc.sbuf_tensor(name, shape, dtype))

        consts = sb("consts", [128, 128], BF16)
        cI = consts[:, 0:128]
        cmask = sb("cmask", [128, 128 * 3], F32)
        cML = cmask[:, 0:128]
        cMU = cmask[:, 128:256]
        cMI = cmask[:, 256:384]
        cOnesColB = sb("conescolb", [128, 1], BF16)
        cOnesColF = sb("conescolf", [128, 1], F32)
        cOnesRowF = sb("conesrowf", [1, 128], F32)
        cNW = sb("cnw", [128, 128], F32)
        cIF = sb("cif", [128, 128], F32)
        R1B = sb("R1B", [128, S], F32)
        RXB = sb("RXB", [128, 1024], F32)
        rows = sb("rows", [1, 3 * S + 2688], F32)
        rows_b = sb("rows_b", [1, S], BF16)
        grow = sb("grow", [32, S * 3], F32)
        g_gc = grow[:, 0:S]
        g_lnb = grow[:, S:2 * S]
        g_scr = grow[:, 2 * S:3 * S]
        rstg = sb("rstg", [1, 256], F32)
        colsA = sb("colsA", [32, 4], F32)
        CC = sb("CC", [128, 8], F32)
        CCvals = [LNEPS, GEPS, RMSEPS, 1.0, -1.0, -LNSQ, 0.0883883476]
        def cb(v):
            return CC[:, CCvals.index(v):CCvals.index(v) + 1]
        def cb1(v):
            return CC[0:1, CCvals.index(v):CCvals.index(v) + 1]
        wbaB = sb("wbaB", [128, 16 * 64], F32)
        bacs = sb("bacs", [1, 64], F32)
        baT = None
        qcs = sb("qcs", [1, 256], BF16)
        colsK = sb("colsK", [128, 4], F32)
        dstage = sb("dstage", [128, 4 * 128], BF16)
        wts = sb("wts", [128, 128 * 4], F32)
        wtsb = sb("wtsb", [128, 128], BF16)
        stk = sb("stk", [128, 128], F32)
        scol = sb("scol", [128, 4], F32)
        eglmg = sb("eglmg", [128, 32], F32)
        egbl_n = sb("egbl_n", [128, 32], F32)
        egcq = sb("egcq", [128, 32], F32)
        eglc = sb("eglc", [128, 32], F32)
        betac = sb("betac", [128, 32], F32)
        gblc = sb("gblc", [128, 32], F32)
        neggc = sb("neggc", [128, 32], F32)
        neggcq = sb("neggcq", [128, 32], F32)

        AR = sb("arena", [128, 50592], BF16)
        def arf32(lo_bytes, n):
            return AR[:, lo_bytes // 2: lo_bytes // 2 + 2 * n].bitcast(F32)
        def arb16(lo_bytes, n):
            return AR[:, lo_bytes // 2: lo_bytes // 2 + n]
        baT = arf32(90112, S)[0:64, :]

        PB = [es.enter_context(nc.psum_tensor("pb%d" % i, [128, 512], F32)) for i in range(8)]
        pb = [buf("pb%d" % i) for i in range(8)]

        for _j, _v in enumerate(CCvals):
            p.op("vector", lambda q, j=_j, v=_v: q.memset(CC[:, j:j + 1], v), writes=[buf("cc")])
        p.op("vector", lambda q: q.memset(cOnesColF[:], 1.0), writes=[buf("onescolf")])
        p.op("sync", lambda q: q.dma_start(out=cI, in_=I128b_t.ap()),
             writes=[buf("consts")], dma=True, dkey="cinit")
        for dst, srct in [(cML, M_L_t), (cMU, M_U_t), (cMI, M_I_t)]:
            p.op("sync", lambda q, d=dst, s=srct: q.dma_start(out=d, in_=s.ap()),
                 writes=[buf("cmask")], dma=True, dkey="cinit")
        p.op("sync", lambda q: q.dma_start(out=cOnesColB[:], in_=onesColB_t.ap()),
             writes=[buf("onescolb")], dma=True, dkey="cinit")
        p.op("sync", lambda q: q.dma_start(out=cOnesRowF[:], in_=onesRowF_t.ap()),
             writes=[buf("onesrowf")], dma=True, dkey="cinit")
        p.op("sync", lambda q: q.dma_start(out=colsA[:, 2:3], in_=Alog),
             writes=[buf("colsA")], dma=True, dkey="cinit")
        p.op("sync", lambda q: q.dma_start(out=colsA[:, 1:2], in_=dtbI),
             writes=[buf("colsA")], dma=True, dkey="cinit")
        nwrow = rows[0:1, 3 * S + 2560:3 * S + 2688]
        p.op("sync", lambda q: q.dma_start(out=nwrow, in_=nwIn),
             writes=[buf("nwrow")], dma=True, dkey="cinit")
        p.op("tensor", lambda q: q.matmul(PB[5][:, 0:128], cOnesRowF[:], nwrow, start=True, stop=True),
             reads=[buf("onesrowf"), buf("nwrow")], writes=[pb[5]])
        p.op("vector", lambda q: q.tensor_copy(cNW[:], PB[5][:, 0:128]),
             reads=[pb[5]], writes=[buf("cnw")])
        p.op("vector", lambda q: q.tensor_copy(cIF[:], cI),
             reads=[buf("consts")], writes=[buf("cif")])
        p.op("scalar", lambda q: q.activation(colsA[:, 3:4], colsA[:, 2:3], AF.Exp),
             reads=[buf("colsA")], writes=[buf("colsA")])
        p.op("vector", lambda q: q.tensor_scalar(out=colsA[:, 0:1], in0=colsA[:, 3:4],
                                                 scalar1=-1.0, scalar2=None, op0=ALU.mult),
             reads=[buf("colsA")], writes=[buf("colsA")])

        xTb = arb16(0, 16 * S)
        xstage = arf32(65536, S)
        xsqb = arb16(94336, S)
        for kt in range(16):
            bx = buf("xstage")
            p.op("sync", lambda q, kt=kt: q.dma_start(out=xstage, in_=xT[kt * 128:(kt + 1) * 128, :]),
                 writes=[bx], dma=True)
            p.op("vector", lambda q, kt=kt: q.tensor_copy(xTb[:, kt * S:(kt + 1) * S], xstage),
                 reads=[bx], writes=[buf("xTb")])
            p.op("scalar", lambda q: q.activation(xsqb, xstage, AF.Square),
                 reads=[bx], writes=[buf("xsqb")])
            for nb in range(4):
                p.op("tensor", lambda q, kt=kt, nb=nb: q.matmul(
                        PB[nb][0:1, 0:512], cOnesColB[:],
                        xTb[:, kt * S + nb * 512:kt * S + (nb + 1) * 512],
                        start=(kt == 0), stop=(kt == 15)),
                     reads=[buf("xTb"), buf("onescolb")], writes=[pb[nb]])
                p.op("tensor", lambda q, kt=kt, nb=nb: q.matmul(
                        PB[4 + nb][0:1, 0:512], cOnesColB[:],
                        xsqb[:, nb * 512:(nb + 1) * 512],
                        start=(kt == 0), stop=(kt == 15)),
                     reads=[buf("xsqb"), buf("onescolb")], writes=[pb[4 + nb]])

        r_rstd1 = rows[0:1, 0:S]
        ln1scr = arf32(73728, 3 * S)[0:1, :]
        r_negmr1 = ln1scr[0:1, 2 * S:3 * S]
        rb_negmr1 = rows_b[0:1, 0:S]

        def rowstats(sumaps, sqaps, width, rstd_dst, negmr_dst, negmr_b_dst, eps, nfeat,
                     rstd_mult=True, scr=None, scr_buf="rows"):
            if scr is None:
                scr = rows[0:1, 3 * S:3 * S + 2560]
            mrow = scr[0:1, 0:width]
            exrow = scr[0:1, width:2 * width]
            vrow = scr[0:1, 2 * width:3 * width]
            sbuf_scr = buf(scr_buf)
            off = 0
            for (a, bq), (a2, bq2) in zip(sumaps, sqaps):
                w = a.shape[-1]
                p.op("vector", lambda q, a=a, off=off, w=w: q.tensor_scalar(
                        out=mrow[0:1, off:off + w], in0=a,
                        scalar1=1.0 / nfeat, scalar2=None, op0=ALU.mult),
                     reads=[bq], writes=[sbuf_scr])
                p.op("vector", lambda q, a2=a2, off=off, w=w: q.tensor_scalar(
                        out=exrow[0:1, off:off + w], in0=a2,
                        scalar1=1.0 / nfeat, scalar2=None, op0=ALU.mult),
                     reads=[bq2], writes=[sbuf_scr])
                off += w
            p.op("vector", lambda q: q.tensor_tensor(out=vrow, in0=mrow, in1=mrow, op=ALU.mult),
                 reads=[sbuf_scr], writes=[sbuf_scr])
            p.op("vector", lambda q: q.tensor_tensor(out=vrow, in0=exrow, in1=vrow, op=ALU.subtract),
                 reads=[sbuf_scr], writes=[sbuf_scr])
            p.op("scalar", lambda q: q.activation(vrow, vrow, AF.Sqrt, bias=cb1(eps)),
                 reads=[sbuf_scr], writes=[sbuf_scr])
            WS = [sbuf_scr, buf("rows"), buf("rows_b")]
            p.op("vector", lambda q: q.reciprocal(rstd_dst, vrow), reads=[sbuf_scr], writes=WS)
            if rstd_mult:
                p.op("vector", lambda q: q.tensor_tensor(out=negmr_dst, in0=mrow, in1=rstd_dst, op=ALU.mult),
                     reads=[sbuf_scr] + WS, writes=WS)
                p.op("vector", lambda q: q.tensor_scalar(out=negmr_dst, in0=negmr_dst, scalar1=-1.0, scalar2=None, op0=ALU.mult),
                     reads=WS, writes=WS)
            else:
                p.op("vector", lambda q: q.tensor_scalar(out=negmr_dst, in0=mrow, scalar1=-1.0, scalar2=None, op0=ALU.mult),
                     reads=[sbuf_scr] + WS, writes=WS)
            if negmr_b_dst is not None:
                p.op("vector", lambda q: q.tensor_scalar(out=negmr_b_dst, in0=mrow, scalar1=-1.0, scalar2=None, op0=ALU.mult),
                     reads=[sbuf_scr] + WS, writes=WS)

        rowstats([(PB[nb][0:1, 0:512], pb[nb]) for nb in range(4)],
                 [(PB[4 + nb][0:1, 0:512], pb[4 + nb]) for nb in range(4)],
                 S, r_rstd1, r_negmr1, rb_negmr1, LNEPS, D, rstd_mult=False,
                 scr=ln1scr[0:1, 0:3 * S], scr_buf="ln1scr")

        def bcast_row(dst, src_row, tag, width):
            for c0 in range(0, width, 512):
                w = min(512, width - c0)
                p.op("tensor", lambda q, c0=c0, w=w: q.matmul(
                        PB[5][:, 0:w], cOnesRowF[:], src_row[0:1, c0:c0 + w], start=True, stop=True),
                     reads=[buf("rows"), buf("onesrowf")], writes=[pb[5]])
                p.op("vector", lambda q, c0=c0, w=w: q.tensor_copy(dst[:, c0:c0 + w], PB[5][:, 0:w]),
                     reads=[pb[5]], writes=[buf(tag)])
        bcast_row(R1B, r_rstd1, "R1B", S)

        for kt in range(16):
            p.op("sync", lambda q, kt=kt: q.dma_start(out=wbaB[:, kt * 64:(kt + 1) * 64],
                                                      in_=Wba[kt * 128:(kt + 1) * 128, :]),
                 writes=[buf("wbaB")], dma=True)
        for kt in range(16):
            p.op("tensor", lambda q, kt=kt: q.matmul(
                    PB[0][0:1, 0:64], cOnesColF[:], wbaB[:, kt * 64:(kt + 1) * 64],
                    start=(kt == 0), stop=(kt == 15)),
                 reads=[buf("wbaB"), buf("onescolf")], writes=[pb[0]])
        p.op("vector", lambda q: q.tensor_copy(bacs[:], PB[0][0:1, 0:64]),
             reads=[pb[0]], writes=[buf("bacs")])
        for kt in range(16):
            p.op("sync", lambda q, kt=kt: q.dma_start(out=xstage, in_=xT[kt * 128:(kt + 1) * 128, :]),
                 writes=[buf("xstage")], dma=True)
            for nb in range(4):
                p.op("tensor", lambda q, kt=kt, nb=nb: q.matmul(
                        PB[4 + nb][0:64, 0:512], wbaB[:, kt * 64:(kt + 1) * 64],
                        xstage[:, nb * 512:(nb + 1) * 512],
                        start=(kt == 0), stop=False),
                     reads=[buf("wbaB"), buf("xstage")], writes=[pb[4 + nb]])
        for nb in range(4):
            p.op("tensor", lambda q, nb=nb: q.matmul(
                    PB[4 + nb][0:64, 0:512], bacs[0:1, :],
                    r_negmr1[0:1, nb * 512:(nb + 1) * 512], start=False, stop=True),
                 reads=[buf("bacs"), buf("ln1scr")], writes=[pb[4 + nb]])
            p.op("vector", lambda q, nb=nb: q.tensor_tensor(
                    out=baT[:, nb * 512:(nb + 1) * 512], in0=PB[4 + nb][0:64, 0:512],
                    in1=R1B[0:64, nb * 512:(nb + 1) * 512], op=ALU.mult),
                 reads=[pb[4 + nb], buf("R1B")], writes=[buf("baT")])
        bg = baT[0:32, :]
        ag = baT[32:64, :]
        p.op("scalar", lambda q: q.activation(g_scr, bg, AF.Abs),
             reads=[buf("baT")], writes=[buf("grow")])
        p.op("scalar", lambda q: q.activation(g_scr, g_scr, AF.Exp, scale=cb(-1.0)[0:32, :]),
             reads=[buf("grow")], writes=[buf("grow")])
        p.op("scalar", lambda q: q.activation(g_scr, g_scr, AF.Ln, bias=cb(1.0)[0:32, :]),
             reads=[buf("grow")], writes=[buf("grow")])
        p.op("vector", lambda q: q.tensor_scalar(out=g_lnb, in0=bg, scalar1=0.0, scalar2=None, op0=ALU.min),
             reads=[buf("baT")], writes=[buf("grow")])
        p.op("vector", lambda q: q.tensor_tensor(out=g_lnb, in0=g_lnb, in1=g_scr, op=ALU.subtract),
             reads=[buf("grow")], writes=[buf("grow")])
        p.op("scalar", lambda q: q.activation(g_scr, ag, AF.Abs, bias=colsA[0:32, 1:2]),
             reads=[buf("baT"), buf("colsA")], writes=[buf("grow")])
        p.op("scalar", lambda q: q.activation(g_scr, g_scr, AF.Exp, scale=cb(-1.0)[0:32, :]),
             reads=[buf("grow")], writes=[buf("grow")])
        p.op("scalar", lambda q: q.activation(g_scr, g_scr, AF.Ln, bias=cb(1.0)[0:32, :]),
             reads=[buf("grow")], writes=[buf("grow")])
        p.op("scalar", lambda q: q.activation(g_gc, ag, AF.Relu, bias=colsA[0:32, 1:2]),
             reads=[buf("baT"), buf("colsA")], writes=[buf("grow")])
        p.op("vector", lambda q: q.tensor_tensor(out=g_gc, in0=g_gc, in1=g_scr, op=ALU.add),
             reads=[buf("grow")], writes=[buf("grow")])
        p.op("vector", lambda q: q.tensor_scalar(out=g_gc, in0=g_gc, scalar1=colsA[0:32, 0:1], scalar2=None, op0=ALU.mult),
             reads=[buf("grow"), buf("colsA")], writes=[buf("grow")])
        p.op("vector", lambda q: q.memset(g_scr[:, 0:UT], 0.0), writes=[buf("grow")])
        for u in range(NU):
            p.op("vector", lambda q, u=u: q.tensor_tensor_scan(
                    out=g_gc[:, u * UT:(u + 1) * UT], data0=g_gc[:, u * UT:(u + 1) * UT],
                    data1=g_scr[:, 0:UT], initial=0.0, op0=ALU.add, op1=ALU.add),
                 reads=[buf("grow")], writes=[buf("grow")])
        g_gbl = g_lnb
        p.op("vector", lambda q: q.tensor_tensor(out=g_gbl, in0=g_gc, in1=g_lnb, op=ALU.add),
             reads=[buf("grow")], writes=[buf("grow")])


        wq0 = arb16(73728, 16 * 256)
        wq1 = arb16(81920, 16 * 256)
        convst = arb16(90112, 2051)
        spst = arb16(94336, S)
        for fb in range(48):
            wq = wq0 if fb % 2 == 0 else wq1
            wqb_ = buf("wq%d" % (fb % 2))
            F0 = fb * 256
            for kt in range(16):
                p.op("sync", lambda q, kt=kt, F0=F0: q.dma_start(
                        out=xstage[:, 0:256], in_=Wqkvz[kt * 128:(kt + 1) * 128, F0:F0 + 256]),
                     writes=[buf("xstage")], dma=True)
                p.op("vector", lambda q, kt=kt, wq=wq: q.tensor_copy(
                        wq[:, kt * 256:(kt + 1) * 256], xstage[:, 0:256]),
                     reads=[buf("xstage")], writes=[wqb_])
            for kt in range(16):
                p.op("tensor", lambda q, kt=kt, wq=wq: q.matmul(
                        PB[5][0:1, 0:256], cOnesColB[:], wq[:, kt * 256:(kt + 1) * 256],
                        start=(kt == 0), stop=(kt == 15)),
                     reads=[wqb_, buf("onescolb")], writes=[pb[5]])
            p.op("vector", lambda q: q.tensor_copy(qcs[:], PB[5][0:1, 0:256]),
                 reads=[pb[5]], writes=[buf("qcs")])
            for ft in range(2):
                ftile = fb * 2 + ft
                r0c = (ftile * 128) % 768
                kh = (ftile * 128) // 768
                if r0c == 0:
                    srow = kh * 128
                    isconv = True
                elif r0c == 128:
                    srow = 2048 + kh * 128
                    isconv = True
                elif r0c < 512:
                    srow = 4096 + kh * 256 + (r0c - 256)
                    isconv = True
                else:
                    srow = 8192 + kh * 256 + (r0c - 512)
                    isconv = False
                if isconv:
                    p.op("sync", lambda q, srow=srow: q.dma_start(
                            out=colsK[:, 0:4], in_=convkT[srow:srow + 128, :]),
                         writes=[buf("colsK")], dma=True)
                    for j in range(4):
                        p.op("vector", lambda q, j=j: q.tensor_scalar(
                                out=dstage[:, j * 128:(j + 1) * 128], in0=cI,
                                scalar1=colsK[:, j:j + 1], scalar2=None, op0=ALU.mult),
                             reads=[buf("colsK"), buf("consts")], writes=[buf("dstage")])
                    p.op("vector", lambda q: q.memset(convst[:, 0:3], 0.0),
                         writes=[buf("convst")])
                for nb in range(4):
                    pbq = nb % 2
                    for kt in range(16):
                        p.op("tensor", lambda q, kt=kt, nb=nb, ft=ft, wq=wq, pbq=pbq: q.matmul(
                                PB[pbq][:, 0:512], wq[:, kt * 256 + ft * 128: kt * 256 + ft * 128 + 128],
                                xTb[:, kt * S + nb * 512:kt * S + (nb + 1) * 512],
                                start=(kt == 0), stop=False),
                             reads=[wqb_, buf("xTb")], writes=[pb[pbq]])
                    p.op("tensor", lambda q, nb=nb, ft=ft, pbq=pbq: q.matmul(
                            PB[pbq][:, 0:512], qcs[0:1, ft * 128:ft * 128 + 128],
                            rows_b[0:1, nb * 512:(nb + 1) * 512], start=False, stop=True),
                         reads=[buf("qcs"), buf("rows_b")], writes=[pb[pbq]])
                    if isconv:
                        p.op("vector", lambda q, nb=nb, pbq=pbq: q.tensor_tensor(
                                out=convst[:, 3 + nb * 512:3 + (nb + 1) * 512],
                                in0=PB[pbq][:, 0:512], in1=R1B[:, nb * 512:(nb + 1) * 512], op=ALU.mult),
                             reads=[pb[pbq], buf("R1B")], writes=[buf("convst")])
                    else:
                        p.op("vector", lambda q, nb=nb, pbq=pbq: q.tensor_tensor(
                                out=spst[:, nb * 512:(nb + 1) * 512],
                                in0=PB[pbq][:, 0:512], in1=R1B[:, nb * 512:(nb + 1) * 512], op=ALU.mult),
                             reads=[pb[pbq], buf("R1B")], writes=[buf("spst")])
                if isconv:
                    for nb in range(4):
                        pbc = 2 + nb % 2
                        for j in range(4):
                            p.op("tensor", lambda q, nb=nb, j=j, pbc=pbc: q.matmul(
                                    PB[pbc][:, 0:512], dstage[:, j * 128:(j + 1) * 128],
                                    convst[:, j + nb * 512: j + nb * 512 + 512],
                                    start=(j == 0), stop=(j == 3)),
                                 reads=[buf("dstage"), buf("convst")], writes=[pb[pbc]])
                        p.op("scalar", lambda q, nb=nb, pbc=pbc: q.activation(
                                spst[:, nb * 512:(nb + 1) * 512], PB[pbc][:, 0:512], AF.Silu),
                             reads=[pb[pbc]], writes=[buf("spst")])
                p.op("sync", lambda q, srow=srow: q.dma_start(
                        out=spill[srow:srow + 128, :], in_=spst),
                     reads=[buf("spst")], writes=[buf("spill")], dma=True)

        wtf = arf32(65536, 2048)
        wtb = arb16(73728, 2048)
        wta = arb16(77824, 2048)
        wtq = arb16(81920, 2048)
        wtt = arb16(86016, 2048)
        def ternarize(src, dst, nrow_tiles, ncol, tag, cs_dst):
            CW = 2048
            ng = CW // 64
            ncolt = CW // 128
            nch = ncol // CW
            for rt in range(nrow_tiles):
                for cc in range(nch):
                    bsrc = buf("wtf")
                    p.op("sync", lambda q, rt=rt, cc=cc: q.dma_start(
                            out=wtf[:, 0:CW],
                            in_=src[rt * 128:(rt + 1) * 128, cc * CW:(cc + 1) * CW]),
                         reads=[buf("spill")], writes=[bsrc], dma=True)
                    p.op("vector", lambda q: q.tensor_copy(wtb[:, 0:CW], wtf[:, 0:CW]),
                         reads=[bsrc], writes=[buf("wtb")])
                    p.op("vector", lambda q: q.tensor_reduce(
                            out=wts[:, 0:ng].rearrange("p (g one) -> p g one", one=1),
                            in_=wtb[:, 0:CW].rearrange("p (g e) -> p g e", e=64),
                            axis=AX.X, op=ALU.add, apply_absolute_value=True),
                         reads=[buf("wtb")], writes=[buf("wts")])
                    p.op("vector", lambda q: q.tensor_scalar(
                            out=wts[:, 128:128 + ng], in0=wts[:, 0:ng], scalar1=1.0 / 64,
                            scalar2=1e-8, op0=ALU.mult, op1=ALU.max),
                         reads=[buf("wts")], writes=[buf("wts")])
                    p.op("vector", lambda q: q.tensor_copy(wtsb[:, 0:ng], wts[:, 128:128 + ng]),
                         reads=[buf("wts")], writes=[buf("wtsb")])
                    p.op("vector", lambda q: q.tensor_copy(wts[:, 128:128 + ng], wtsb[:, 0:ng]),
                         reads=[buf("wtsb")], writes=[buf("wts")])
                    p.op("vector", lambda q: q.tensor_scalar(out=wts[:, 256:256 + ng], in0=wts[:, 128:128 + ng],
                                        scalar1=0.5, scalar2=None, op0=ALU.mult),
                         reads=[buf("wts")], writes=[buf("wts")])
                    p.op("vector", lambda q: q.tensor_scalar(out=wts[:, 384:384 + ng], in0=wts[:, 256:256 + ng],
                                        scalar1=-1.0, scalar2=None, op0=ALU.mult),
                         reads=[buf("wts")], writes=[buf("wts")])
                    def terna(q):
                        wv = wtb[:, 0:CW].rearrange("p (g e) -> p g e", e=64)
                        Hv = wts[:, 256:256 + ng].rearrange("p (g one) -> p g one", one=1)
                        return q.tensor_tensor(out=wta[:, 0:CW].rearrange("p (g e) -> p g e", e=64),
                                        in0=wv, in1=Hv.broadcast_to([128, ng, 64]), op=ALU.is_gt)
                    p.op("vector", terna, reads=[buf("wtb"), buf("wts")], writes=[buf("wta")])
                    def ternb(q):
                        wv = wtb[:, 0:CW].rearrange("p (g e) -> p g e", e=64)
                        nHv = wts[:, 384:384 + ng].rearrange("p (g one) -> p g one", one=1)
                        return q.tensor_tensor(out=wtq[:, 0:CW].rearrange("p (g e) -> p g e", e=64),
                                        in0=wv, in1=nHv.broadcast_to([128, ng, 64]), op=ALU.is_lt)
                    p.op("vector", ternb, reads=[buf("wtb"), buf("wts")], writes=[buf("wtq")])
                    p.op("vector", lambda q: q.tensor_tensor(
                            out=wta[:, 0:CW], in0=wta[:, 0:CW], in1=wtq[:, 0:CW], op=ALU.subtract),
                         reads=[buf("wta"), buf("wtq")], writes=[buf("wta")])
                    def ternc(q):
                        sv = wts[:, 128:128 + ng].rearrange("p (g one) -> p g one", one=1)
                        return q.tensor_tensor(out=wtq[:, 0:CW].rearrange("p (g e) -> p g e", e=64),
                                        in0=wta[:, 0:CW].rearrange("p (g e) -> p g e", e=64),
                                        in1=sv.broadcast_to([128, ng, 64]), op=ALU.mult)
                    p.op("vector", ternc, reads=[buf("wta"), buf("wts")], writes=[buf("wtq")])
                    for ct in range(ncolt):
                        pbt = 6 + ct % 2
                        p.op("tensor", lambda q, ct=ct, pbt=pbt: q.matmul(
                                PB[pbt][:, 0:128], wtq[:, ct * 128:(ct + 1) * 128], cI,
                                start=True, stop=True),
                             reads=[buf("wtq"), buf("consts")], writes=[pb[pbt]])
                        p.op("scalar", lambda q, ct=ct, pbt=pbt: q.activation(
                                wtt[:, ct * 128:(ct + 1) * 128], PB[pbt][:, 0:128], AF.Copy),
                             reads=[pb[pbt]], writes=[buf("wtt")])
                    p.op("sync", lambda q, rt=rt, cc=cc: q.dma_start(
                            out=dst.rearrange("(a pp) r -> pp a r", pp=128)[
                                :, cc * ncolt:(cc + 1) * ncolt, rt * 128:(rt + 1) * 128],
                            in_=wtt[:, 0:CW].rearrange("p (a c) -> p a c", c=128)),
                         reads=[buf("wtt")], writes=[buf(tag)], dma=True)
        ternarize(Wfc, w1t_s, 64, 2048, "w1t", None)
        ternarize(Wproj, w2t_s, 16, 8192, "w2t", None)
        for rt in range(32):
            p.op("sync", lambda q, rt=rt: q.dma_start(out=wtf[:, 0:2048],
                                                      in_=Wout[rt * 128:(rt + 1) * 128, :]),
                 writes=[buf("wtf")], dma=True)
            p.op("vector", lambda q: q.tensor_copy(wtb[:, 0:2048], wtf[:, 0:2048]),
                 reads=[buf("wtf")], writes=[buf("wtb")])
            p.op("sync", lambda q, rt=rt: q.dma_start(out=wout_s[rt * 128:(rt + 1) * 128, :],
                                                      in_=wtb[:, 0:2048]),
                 reads=[buf("wtb")], writes=[buf("wout_s")], dma=True)

        xfblock = arb16(0, 32 * 4 * 128)
        Sf = arf32(32768, 32 * 128)
        Sb = arb16(49152, 32 * 128)
        kqbuf = arb16(57344, 2 * 256)
        vzbuf = arb16(58368, 2 * 256)
        HT0 = 65536
        _roles = {}
        _ro = 0
        for _r in ["L", "U", "P0", "P1", "P2", "P3", "P4", "P5", "W", "E", "tD",
                   "aT", "kgl", "r0", "vf", "tm", "u1", "Xv2"]:
            _roles[_r] = (_ro, True); _ro += 512
        for _r in ["Xv", "zs", "xf", "vb"]:
            _roles[_r] = (_ro, False); _ro += 256
        HSLOT = _ro
        def hrole(slotid, name):
            off, isf = _roles[name]
            off = HT0 + slotid * HSLOT + off
            if isf:
                return AR[:, off // 2: off // 2 + 256].bitcast(F32)
            return AR[:, off // 2: off // 2 + 128]
        P3A = HT0
        attnb = arb16(P3A, 16 * 512)
        sqst = arb16(P3A + 16384, 512)
        x1st = arf32(P3A + 16384 + 1024, 512)
        wostage0 = arb16(P3A + 16384 + 3072, 1024)
        wostage1 = arb16(P3A + 16384 + 5120, 1024)

        p.op("vector", lambda q: q.memset(Sf[:, :], 0.0), reads=[buf("spill")], writes=[buf("Sf")])
        p.op("vector", lambda q: q.memset(Sb[:, :], 0.0), reads=[buf("spill")], writes=[buf("Sb")])

        r_rstd2 = rows[0:1, S:2 * S]
        r_negm2 = rows[0:1, 2 * S:3 * S]
        r_cffn = rows[0:1, 3 * S + 1536:3 * S + 2048]

        for u in range(NU):
            T0 = u * UT
            ub = u % 4
            blk = u // 4
            p.op("vector", lambda q, T0=T0: q.tensor_copy(stk[0:32, 0:128], g_gc[:, T0:T0 + UT]),
                 reads=[buf("grow")], writes=[buf("stk")])
            p.op("vector", lambda q, T0=T0: q.tensor_copy(stk[32:64, 0:128], g_gbl[:, T0:T0 + UT]),
                 reads=[buf("grow")], writes=[buf("stk")])
            p.op("vector", lambda q, T0=T0: q.tensor_scalar(out=stk[64:96, 0:128], in0=g_gc[:, T0:T0 + UT],
                            scalar1=g_gc[:, T0 + UT - 1:T0 + UT], scalar2=None, op0=ALU.subtract),
                 reads=[buf("grow")], writes=[buf("stk")])
            p.op("vector", lambda q: q.memset(stk[96:128, 0:128], 0.0),
                 reads=[], writes=[buf("stk")])
            p.op("tensor", lambda q: q.matmul(PB[1][:, 0:128], stk[:, 0:128], cIF[:], start=True, stop=True),
                 reads=[buf("stk"), buf("cif")], writes=[pb[1]])
            p.op("scalar", lambda q: q.activation(gblc[:], PB[1][:, 32:64], AF.Copy),
                 reads=[pb[1]], writes=[buf("gblc")])
            p.op("scalar", lambda q: q.activation(neggc[:], PB[1][:, 0:32], AF.Copy, scale=cb(-1.0)),
                 reads=[pb[1]], writes=[buf("neggc")])
            p.op("vector", lambda q: q.tensor_scalar(out=neggcq[:], in0=neggc[:], scalar1=-LNSQ,
                                                     scalar2=None, op0=ALU.add),
                 reads=[buf("neggc")], writes=[buf("neggcq")])
            p.op("scalar", lambda q: q.activation(egcq[:], PB[1][:, 0:32], AF.Exp, bias=cb(-LNSQ)),
                 reads=[pb[1]], writes=[buf("egcq")])
            p.op("scalar", lambda q: q.activation(egbl_n[:], PB[1][:, 32:64], AF.Exp),
                 reads=[pb[1]], writes=[buf("egbl_n")])
            p.op("scalar", lambda q: q.activation(eglmg[:], PB[1][:, 64:96], AF.Exp, scale=cb(-1.0)),
                 reads=[pb[1]], writes=[buf("eglmg")])
            p.op("scalar", lambda q: q.activation(eglc[:], PB[1][:, 64:96], AF.Copy),
                 reads=[pb[1]], writes=[buf("eglc")])
            p.op("vector", lambda q: q.tensor_tensor(out=eglc[:], in0=eglc[:],
                                                     in1=neggc[:], op=ALU.add),
                 reads=[buf("eglc"), buf("neggc")], writes=[buf("eglc")])
            p.op("scalar", lambda q: q.activation(eglc[:], eglc[:], AF.Exp, scale=cb(-1.0)),
                 reads=[buf("eglc")], writes=[buf("eglc")])
            p.op("vector", lambda q: q.tensor_tensor(out=betac[:], in0=gblc[:],
                                                     in1=neggc[:], op=ALU.add),
                 reads=[buf("gblc"), buf("neggc")], writes=[buf("betac")])
            p.op("scalar", lambda q: q.activation(betac[:], betac[:], AF.Exp),
                 reads=[buf("betac")], writes=[buf("betac")])
            p.op("vector", lambda q: q.tensor_scalar(out=egbl_n[:], in0=egbl_n[:], scalar1=-1.0,
                                                     scalar2=None, op0=ALU.mult),
                 reads=[buf("egbl_n")], writes=[buf("egbl_n")])

            for vh in range(NVH):
                kh = vh // 2
                hs = vh % 2
                ks = kh % 2
                kq = kqbuf[:, ks * 256:ks * 256 + 256]
                bkq = buf("kq%d" % ks)
                if vh % 2 == 0:
                    p.op("gpsimd", lambda q, kh=kh, T0=T0, kq=kq: q.dma_start(
                            out=kq[:, 0:128], in_=spill[2048 + kh * 128:2048 + (kh + 1) * 128, T0:T0 + UT]),
                         reads=[buf("spill"), buf("w1t"), buf("w2t"), buf("wout_s")],
                         writes=[bkq, buf("attnb"), buf("x1st"), buf("sqst"), buf("wo0"), buf("wo1")],
                         dma=True)
                    p.op("gpsimd", lambda q, kh=kh, T0=T0, kq=kq: q.dma_start(
                            out=kq[:, 128:256], in_=spill[kh * 128:(kh + 1) * 128, T0:T0 + UT]),
                         reads=[buf("spill")], writes=[bkq], dma=True)
                vz = vzbuf[:, hs * 256:hs * 256 + 256]
                bvz = buf("vz%d" % hs)
                p.op("gpsimd", lambda q, vh=vh, T0=T0, vz=vz: q.dma_start(
                        out=vz[:, 0:128], in_=spill[4096 + vh * 128:4096 + (vh + 1) * 128, T0:T0 + UT]),
                     reads=[buf("spill")],
                     writes=[bvz, buf("attnb"), buf("x1st"), buf("sqst"), buf("wo0"), buf("wo1")],
                     dma=True)
                p.op("gpsimd", lambda q, vh=vh, T0=T0, vz=vz: q.dma_start(
                        out=vz[:, 128:256], in_=spill[8192 + vh * 128:8192 + (vh + 1) * 128, T0:T0 + UT]),
                     reads=[buf("spill")], writes=[bvz], dma=True)
                kF = kq[:, 0:128]
                qF = kq[:, 128:256]
                vF = vz[:, 0:128]
                zF = vz[:, 128:256]

                tL = hrole(hs, "L"); tU = hrole(hs, "U")
                tP = [hrole(hs, "P%d" % i_) for i_ in range(6)]
                tW = hrole(hs, "W"); taT = hrole(hs, "aT"); tXv = hrole(hs, "Xv2")
                tkgl = hrole(hs, "kgl")
                tzs = hrole(hs, "zs"); tr0 = hrole(hs, "r0"); txf = hrole(hs, "xf")
                tm = hrole(hs, "tm")
                u1 = hrole(hs, "u1")
                tD = hrole(hs, "tD")
                tE = hrole(hs, "E")
                bh = buf("ht%d" % hs)

                p.op("tensor", lambda q, kF=kF: q.matmul(PB[2][:, 0:128], kF, cI, start=True, stop=True),
                     reads=[bkq, buf("consts")], writes=[pb[2]])
                p.op("tensor", lambda q, vF=vF: q.matmul(PB[2][:, 128:256], vF, cI, start=True, stop=True),
                     reads=[bvz, buf("consts")], writes=[pb[2]])
                p.op("tensor", lambda q, zF=zF: q.matmul(PB[2][:, 256:384], zF, cI, start=True, stop=True),
                     reads=[bvz, buf("consts")], writes=[pb[2]])
                p.op("scalar", lambda q, tXv=tXv, vh=vh: q.activation(
                        tXv, PB[2][:, 128:256], AF.Copy, scale=betac[:, vh:vh + 1]),
                     reads=[pb[2], buf("betac")], writes=[bh])
                p.op("scalar", lambda q, tkgl=tkgl, vh=vh: q.activation(
                        tkgl, PB[2][:, 0:128], AF.Copy, scale=eglmg[:, vh:vh + 1]),
                     reads=[pb[2], buf("eglmg")], writes=[bh])
                p.op("scalar", lambda q, tzs=tzs: q.activation(tzs, PB[2][:, 256:384], AF.Silu),
                     reads=[pb[2]], writes=[bh])

                p.op("tensor", lambda q, kF=kF, kq=kq: q.matmul(PB[0][:, 0:256], kF, kq[:, 0:256],
                                                                start=True, stop=True),
                     reads=[bkq], writes=[pb[0]])

                p.op("sync", lambda q, vh=vh, T0=T0: q.dma_start(
                        out=rstg[0:1, 0:256],
                        in_=grow[vh:vh + 1, :].rearrange("o (a s) -> o a s", a=3)[:, 0:2, T0:T0 + UT]),
                     reads=[buf("grow")], writes=[buf("rstg")], dma=True)
                def dec_build(qcol0, rsl, mask, bias_ap, scale, bias_buf, vh=vh, bh=bh, tD=tD):
                    p.op("tensor", lambda q, rsl=rsl, qcol0=qcol0: q.matmul(
                            PB[1][:, qcol0:qcol0 + 128], cOnesRowF[:],
                            rstg[0:1, rsl * 128:(rsl + 1) * 128], start=True, stop=False),
                         reads=[buf("rstg"), buf("onesrowf")], writes=[pb[1]])
                    p.op("tensor", lambda q, mask=mask, qcol0=qcol0: q.matmul(
                            PB[1][:, qcol0:qcol0 + 128], cIF[:], mask, start=False, stop=True),
                         reads=[buf("cmask"), buf("cif")], writes=[pb[1]])
                    p.op("scalar", lambda q, bias_ap=bias_ap, scale=scale, qcol0=qcol0:
                            q.activation(tD, PB[1][:, qcol0:qcol0 + 128], AF.Exp,
                                         bias=bias_ap, scale=scale),
                         reads=[pb[1], buf(bias_buf)], writes=[bh])
                dec_build(0, 0, cML, gblc[:, vh:vh + 1], -1.0, "gblc")
                p.op("vector", lambda q, tL=tL, tD=tD: q.scalar_tensor_tensor(
                        out=tL, in0=PB[0][:, 0:128], scalar=-1.0, in1=tD,
                        op0=ALU.mult, op1=ALU.mult),
                     reads=[pb[0], bh], writes=[bh])
                dec_build(128, 1, cMU, neggc[:, vh:vh + 1], 1.0, "neggc")
                p.op("vector", lambda q, tU=tU, tD=tD: q.scalar_tensor_tensor(
                        out=tU, in0=PB[0][:, 0:128], scalar=-1.0, in1=tD,
                        op0=ALU.mult, op1=ALU.mult),
                     reads=[pb[0], bh], writes=[bh])
                dec_build(256, 0, cMI, neggcq[:, vh:vh + 1], 1.0, "neggcq")
                p.op("vector", lambda q, taT=taT, tD=tD: q.scalar_tensor_tensor(
                        out=taT, in0=PB[0][:, 128:256], scalar=-1.0, in1=tD,
                        op0=ALU.mult, op1=ALU.mult),
                     reads=[pb[0], bh], writes=[bh])

                srcL, srcU = tL, tU
                for lev in range(3):
                    dL, dU = tP[2 * lev], tP[2 * lev + 1]
                    qc = (lev % 2) * 256
                    p.op("tensor", lambda q, srcL=srcL, srcU=srcU, qc=qc: q.matmul(
                            PB[3][:, qc:qc + 128], srcU, srcL, start=True, stop=True),
                         reads=[bh], writes=[pb[3]])
                    p.op("tensor", lambda q, srcL=srcL, srcU=srcU, qc=qc: q.matmul(
                            PB[3][:, qc + 128:qc + 256], srcL, srcU, start=True, stop=True),
                         reads=[bh], writes=[pb[3]])
                    p.op("vector", lambda q, dL=dL, qc=qc: q.tensor_copy(dL, PB[3][:, qc:qc + 128]),
                         reads=[pb[3]], writes=[bh])
                    p.op("vector", lambda q, dU=dU, qc=qc: q.tensor_copy(dU, PB[3][:, qc + 128:qc + 256]),
                         reads=[pb[3]], writes=[bh])
                    srcL, srcU = dL, dU
                p.op("vector", lambda q, tW=tW, tU=tU: q.tensor_tensor(out=tW, in0=tU, in1=cIF[:], op=ALU.add),
                     reads=[bh, buf("cif")], writes=[bh])
                for lev in range(3):
                    Lp = tP[2 * lev]
                    qc = (lev % 2) * 256 + 128
                    p.op("vector", lambda q, tE=tE, Lp=Lp: q.tensor_tensor(out=tE, in0=Lp, in1=cIF[:], op=ALU.add),
                         reads=[bh, buf("cif")], writes=[bh])
                    p.op("tensor", lambda q, tE=tE, tW=tW, qc=qc: q.matmul(
                            PB[3][:, qc:qc + 128], tE, tW, start=True, stop=True),
                         reads=[bh], writes=[pb[3]])
                    p.op("vector", lambda q, tW=tW, qc=qc: q.tensor_copy(tW, PB[3][:, qc:qc + 128]),
                         reads=[pb[3]], writes=[bh])
                L8, U8 = tP[4], tP[5]
                tL16, tU16 = tP[0], tP[1]
                p.op("tensor", lambda q, U8=U8, L8=L8: q.matmul(PB[3][:, 0:128], U8, L8, start=True, stop=True),
                     reads=[bh], writes=[pb[3]])
                p.op("tensor", lambda q, U8=U8, L8=L8: q.matmul(PB[3][:, 128:256], L8, U8, start=True, stop=True),
                     reads=[bh], writes=[pb[3]])
                p.op("vector", lambda q, tL16=tL16: q.tensor_copy(tL16, PB[3][:, 0:128]),
                     reads=[pb[3]], writes=[bh])
                p.op("vector", lambda q, tU16=tU16: q.tensor_copy(tU16, PB[3][:, 128:256]),
                     reads=[pb[3]], writes=[bh])
                tL32, tU32 = tP[2], tP[3]
                p.op("tensor", lambda q, tU16=tU16, tL16=tL16: q.matmul(
                        PB[3][:, 256:384], tU16, tL16, start=True, stop=True),
                     reads=[bh], writes=[pb[3]])
                p.op("tensor", lambda q, tU16=tU16, tL16=tL16: q.matmul(
                        PB[3][:, 384:512], tL16, tU16, start=True, stop=True),
                     reads=[bh], writes=[pb[3]])
                p.op("vector", lambda q, tL32=tL32: q.tensor_copy(tL32, PB[3][:, 256:384]),
                     reads=[pb[3]], writes=[bh])
                p.op("vector", lambda q, tU32=tU32: q.tensor_copy(tU32, PB[3][:, 384:512]),
                     reads=[pb[3]], writes=[bh])
                tU64 = tP[4]
                p.op("tensor", lambda q, tU32=tU32, tL32=tL32: q.matmul(
                        PB[3][:, 0:128], tL32, tU32, start=True, stop=True),
                     reads=[bh], writes=[pb[3]])
                p.op("vector", lambda q, tU64=tU64: q.tensor_copy(tU64, PB[3][:, 0:128]),
                     reads=[pb[3]], writes=[bh])

                p.op("tensor", lambda q, kF=kF, vh=vh: q.matmul(
                        PB[6][:, 0:128], kF, Sb[:, vh * 128:(vh + 1) * 128], start=True, stop=True),
                     reads=[bkq, buf("Sb")], writes=[pb[6]])
                p.op("vector", lambda q, tr0=tr0, tXv=tXv, vh=vh: q.scalar_tensor_tensor(
                        out=tr0, in0=PB[6][:, 0:128], scalar=egbl_n[:, vh:vh + 1], in1=tXv,
                        op0=ALU.mult, op1=ALU.add),
                     reads=[pb[6], bh, buf("egbl_n")], writes=[bh])
                tvf = hrole(hs, "vf")
                p.op("tensor", lambda q, tW=tW, tr0=tr0: q.matmul(PB[4][:, 0:128], tW, tr0,
                                                                  start=True, stop=True),
                     reads=[bh], writes=[pb[4]])
                p.op("vector", lambda q, tvf=tvf: q.tensor_copy(tvf, PB[4][:, 0:128]),
                     reads=[pb[4]], writes=[bh])
                for (Upow, qc) in [(tU16, 128), (tU32, 256), (tU64, 384)]:
                    p.op("tensor", lambda q, Upow=Upow, tvf=tvf, qc=qc: q.matmul(
                            PB[4][:, qc:qc + 128], cIF[:], tvf, start=True, stop=False),
                         reads=[bh, buf("cif")], writes=[pb[4]])
                    p.op("tensor", lambda q, Upow=Upow, tvf=tvf, qc=qc: q.matmul(
                            PB[4][:, qc:qc + 128], Upow, tvf, start=False, stop=True),
                         reads=[bh], writes=[pb[4]])
                    p.op("vector", lambda q, tvf=tvf, qc=qc: q.tensor_copy(tvf, PB[4][:, qc:qc + 128]),
                         reads=[pb[4]], writes=[bh])

                p.op("tensor", lambda q, taT=taT, tvf=tvf: q.matmul(
                        PB[5][:, 0:128], taT, tvf, start=True, stop=True),
                     reads=[bh], writes=[pb[5]])
                p.op("tensor", lambda q, qF=qF, vh=vh: q.matmul(
                        PB[6][:, 128:256], qF, Sb[:, vh * 128:(vh + 1) * 128], start=True, stop=True),
                     reads=[bkq, buf("Sb")], writes=[pb[6]])
                p.op("scalar", lambda q, u1=u1, vh=vh: q.activation(
                        u1, PB[6][:, 128:256], AF.Copy, scale=egcq[:, vh:vh + 1]),
                     reads=[pb[6], buf("egcq")], writes=[bh])
                p.op("vector", lambda q, tm=tm, u1=u1: q.tensor_tensor(
                        out=tm, in0=u1, in1=PB[5][:, 0:128], op=ALU.subtract),
                     reads=[pb[5], bh], writes=[bh])
                p.op("vector", lambda q, tm=tm, tzs=tzs: q.tensor_tensor(out=tm, in0=tm, in1=tzs, op=ALU.mult),
                     reads=[bh], writes=[bh])
                p.op("scalar", lambda q, u1=u1, tm=tm: q.activation(u1, tm, AF.Square, scale=cb(0.0883883476)),
                     reads=[bh], writes=[bh])
                p.op("vector", lambda q, u1=u1: q.tensor_reduce(out=scol[:, 0:1], in_=u1,
                                                                axis=AX.X, op=ALU.add),
                     reads=[bh], writes=[buf("scol")])
                p.op("scalar", lambda q: q.activation(scol[:, 1:2], scol[:, 0:1], AF.Sqrt, bias=cb(GEPS)),
                     reads=[buf("scol")], writes=[buf("scol")])
                p.op("vector", lambda q: q.reciprocal(scol[:, 2:3], scol[:, 1:2]),
                     reads=[buf("scol")], writes=[buf("scol")])
                p.op("vector", lambda q, tm=tm: q.tensor_scalar(out=tm, in0=tm, scalar1=scol[:, 2:3],
                                                                scalar2=None, op0=ALU.mult),
                     reads=[bh, buf("scol")], writes=[bh])
                p.op("vector", lambda q, txf=txf, tm=tm: q.tensor_tensor(out=txf, in0=tm, in1=cNW[:, 0:128],
                                                                         op=ALU.mult),
                     reads=[bh, buf("cnw")], writes=[bh])
                p.op("tensor", lambda q, txf=txf: q.matmul(PB[7][:, 0:128], txf, cI, start=True, stop=True),
                     reads=[bh, buf("consts")], writes=[pb[7]])
                p.op("vector", lambda q, vh=vh, ub=ub: q.tensor_copy(
                        xfblock[:, vh * 512 + ub * 128: vh * 512 + (ub + 1) * 128], PB[7][:, 0:128]),
                     reads=[pb[7]], writes=[buf("xfblock")])
                p.op("tensor", lambda q, tkgl=tkgl, tvf=tvf: q.matmul(
                        PB[6][:, 256:384], tkgl, tvf, start=True, stop=True),
                     reads=[bh], writes=[pb[6]])
                p.op("vector", lambda q, vh=vh: q.scalar_tensor_tensor(
                        out=Sf[:, vh * 128:(vh + 1) * 128], in0=Sf[:, vh * 128:(vh + 1) * 128],
                        scalar=eglc[:, vh:vh + 1], in1=PB[6][:, 256:384], op0=ALU.mult, op1=ALU.add),
                     reads=[pb[6], buf("eglc"), buf("Sf")], writes=[buf("Sf")])
                p.op("vector", lambda q, vh=vh: q.tensor_copy(Sb[:, vh * 128:(vh + 1) * 128],
                                                              Sf[:, vh * 128:(vh + 1) * 128]),
                     reads=[buf("Sf")], writes=[buf("Sb")])

            if ub == 3:
                B0 = blk * 512
                for dhalf in range(2):
                    for k in range(32):
                        wo = wostage0 if k % 2 == 0 else wostage1
                        bwo = buf("wo%d" % (k % 2))
                        p.op("gpsimd", lambda q, k=k, dhalf=dhalf, wo=wo: q.dma_start(
                                out=wo[:, 0:1024],
                                in_=wout_s[k * 128:(k + 1) * 128, dhalf * 1024:(dhalf + 1) * 1024]),
                             writes=[bwo, buf("ht0"), buf("ht1")], dma=True)
                        for dt in range(8):
                            p.op("tensor", lambda q, k=k, dt=dt, wo=wo: q.matmul(
                                    PB[dt][:, 0:512], wo[:, dt * 128:(dt + 1) * 128],
                                    xfblock[:, k * 512:(k + 1) * 512],
                                    start=(k == 0), stop=(k == 31)),
                                 reads=[bwo, buf("xfblock")], writes=[pb[dt]])
                    for dt in range(8):
                        p.op("scalar", lambda q, dt=dt, dhalf=dhalf: q.activation(
                                attnb[:, (dhalf * 8 + dt) * 512:(dhalf * 8 + dt + 1) * 512],
                                PB[dt][:, 0:512], AF.Copy),
                             reads=[pb[dt]], writes=[buf("attnb"), buf("ht0"), buf("ht1")])
                for vh_ in range(32):
                    p.op("sync", lambda q, vh_=vh_, B0=B0: q.dma_start(
                            out=dbg_xf[vh_ * 128:(vh_ + 1) * 128, B0:B0 + 512],
                            in_=xfblock[:, vh_ * 512:(vh_ + 1) * 512]),
                         reads=[buf("xfblock")], writes=[buf("dbg_xf")], dma=True)
                for dt_ in range(16):
                    p.op("sync", lambda q, dt_=dt_, B0=B0: q.dma_start(
                            out=dbg_at[dt_ * 128:(dt_ + 1) * 128, B0:B0 + 512],
                            in_=attnb[:, dt_ * 512:(dt_ + 1) * 512]),
                         reads=[buf("attnb")], writes=[buf("dbg_at")], dma=True)
                for dt in range(16):
                    p.op("scalar", lambda q, dt=dt: q.activation(
                            sqst, attnb[:, dt * 512:(dt + 1) * 512], AF.Square),
                         reads=[buf("attnb")], writes=[buf("sqst")])
                    p.op("tensor", lambda q, dt=dt: q.matmul(
                            PB[0][0:1, 0:512], cOnesColB[:], attnb[:, dt * 512:(dt + 1) * 512],
                            start=(dt == 0), stop=(dt == 15)),
                         reads=[buf("attnb"), buf("onescolb")], writes=[pb[0]])
                    p.op("tensor", lambda q, dt=dt: q.matmul(
                            PB[1][0:1, 0:512], cOnesColB[:], sqst,
                            start=(dt == 0), stop=(dt == 15)),
                         reads=[buf("sqst"), buf("onescolb")], writes=[pb[1]])
                ra_rstd = rows[0:1, 3 * S + 1536:3 * S + 2048]
                ra_negmr = rows[0:1, 3 * S + 2048:3 * S + 2560]
                rowstats([(PB[0][0:1, 0:512], pb[0])], [(PB[1][0:1, 0:512], pb[1])],
                         512, ra_rstd, ra_negmr, None, LNEPS, D)
                bcast_row(RXB[:, 0:512], ra_rstd, "RXB", 512)
                bcast_row(RXB[:, 512:1024], ra_negmr, "RXB2", 512)
                for dt in range(16):
                    p.op("sync", lambda q, dt=dt, B0=B0: q.dma_start(
                            out=x1st, in_=xT[dt * 128:(dt + 1) * 128, B0:B0 + 512]),
                         writes=[buf("x1st"), buf("ht0"), buf("ht1")], dma=True)
                    p.op("vector", lambda q, dt=dt: q.tensor_tensor(
                            out=x1st, in0=x1st, in1=RXB[:, 512:1024], op=ALU.add),
                         reads=[buf("x1st"), buf("RXB2")], writes=[buf("x1st")])
                    def comb(q, dt=dt):
                        return q.scalar_tensor_tensor(
                            out=x1st, in0=attnb[:, dt * 512:(dt + 1) * 512], scalar=1.0,
                            in1=x1st, op0=ALU.mult, op1=ALU.mult)
                    p.op("vector", lambda q, dt=dt: q.tensor_tensor(
                            out=sqst, in0=attnb[:, dt * 512:(dt + 1) * 512],
                            in1=RXB[:, 0:512], op=ALU.mult),
                         reads=[buf("attnb"), buf("RXB")], writes=[buf("sqst")])
                    p.op("vector", lambda q, dt=dt: q.tensor_tensor(
                            out=x1st, in0=x1st, in1=sqst, op=ALU.add),
                         reads=[buf("sqst"), buf("x1st")], writes=[buf("x1st")])
                    p.op("sync", lambda q, dt=dt, B0=B0: q.dma_start(
                            out=x1f_s[dt * 128:(dt + 1) * 128, B0:B0 + 512], in_=x1st),
                         reads=[buf("x1st")], writes=[buf("x1f_s")], dma=True)
                    p.op("vector", lambda q: q.tensor_copy(sqst, x1st),
                         reads=[buf("x1st")], writes=[buf("sqst")])
                    p.op("tensor", lambda q, dt=dt: q.matmul(
                            PB[2][0:1, 0:512], cOnesColB[:], sqst, start=(dt == 0), stop=(dt == 15)),
                         reads=[buf("sqst"), buf("onescolb")], writes=[pb[2]])
                    p.op("scalar", lambda q, dt=dt: q.activation(
                            attnb[:, dt * 512:(dt + 1) * 512], sqst, AF.Square),
                         reads=[buf("sqst")], writes=[buf("attnb")])
                    p.op("tensor", lambda q, dt=dt: q.matmul(
                            PB[3][0:1, 0:512], cOnesColB[:], attnb[:, dt * 512:(dt + 1) * 512],
                            start=(dt == 0), stop=(dt == 15)),
                         reads=[buf("attnb"), buf("onescolb")], writes=[pb[3]])
                rowstats([(PB[2][0:1, 0:512], pb[2])], [(PB[3][0:1, 0:512], pb[3])],
                         512, rows[0:1, S + B0:S + B0 + 512],
                         rows[0:1, 2 * S + B0:2 * S + B0 + 512],
                         None, LNEPS, D, rstd_mult=False)
                bcast_row(RXB[:, 512:1024], rows[0:1, 2 * S + B0:2 * S + B0 + 512], "RXBm2", 512)
                for dt in range(16):
                    p.op("sync", lambda q, dt=dt, B0=B0: q.dma_start(
                            out=x1st, in_=x1f_s[dt * 128:(dt + 1) * 128, B0:B0 + 512]),
                         reads=[buf("x1f_s")], writes=[buf("x1st")], dma=True)
                    p.op("vector", lambda q: q.tensor_tensor(
                            out=sqst, in0=x1st, in1=RXB[:, 512:1024], op=ALU.add),
                         reads=[buf("x1st"), buf("RXBm2")], writes=[buf("sqst")])
                    p.op("sync", lambda q, dt=dt, B0=B0: q.dma_start(
                            out=x1b_s[dt * 128:(dt + 1) * 128, B0:B0 + 512], in_=sqst),
                         reads=[buf("sqst")], writes=[buf("x1b_s")], dma=True)


        x1stage = arb16(0, 16 * 512)
        Har = arb16(16384, 64 * 512)
        w1st0 = arb16(81920, 16 * 128)
        w1st1 = arb16(86016, 16 * 128)
        w2st0 = arb16(81920, 64 * 128)
        w2st1 = w2st0
        ffnb = arb16(0, 16 * 512)
        fsq = arb16(98304, 512)
        fx = arf32(16384, 512)

        p.op("vector", lambda q: q.engine_nop(), writes=[buf("attnb"), buf("xfblock"),
             buf("x1st"), buf("sqst"), buf("wo0"), buf("wo1"), buf("ht0"), buf("ht1")])
        for blk in range(4):
            B0 = blk * 512
            for dt in range(16):
                p.op("sync", lambda q, dt=dt, B0=B0: q.dma_start(
                        out=x1stage[:, dt * 512:(dt + 1) * 512],
                        in_=x1b_s[dt * 128:(dt + 1) * 128, B0:B0 + 512]),
                     reads=[buf("xfblock"), buf("x1b_s")], writes=[buf("x1stage")], dma=True)
            for ht in range(64):
                wst = w1st0 if ht % 2 == 0 else w1st1
                bws = buf("w1s%d" % (ht % 2))
                p.op("gpsimd", lambda q, ht=ht, wst=wst: q.dma_start(
                        out=wst[:, 0:2048].rearrange("p (a c) -> p a c", c=128),
                        in_=w1t_s.rearrange("(a pp) h -> pp a h", pp=128)[:, :, ht * 128:(ht + 1) * 128]),
                     reads=[buf("attnb"), buf("w1t")], writes=[bws], dma=True)
                pbq = ht % 2
                for kt in range(16):
                    p.op("tensor", lambda q, kt=kt, ht=ht, wst=wst, pbq=pbq: q.matmul(
                            PB[pbq][:, 0:512], wst[:, kt * 128:(kt + 1) * 128],
                            x1stage[:, kt * 512:(kt + 1) * 512],
                            start=(kt == 0), stop=(kt == 15)),
                         reads=[bws, buf("x1stage")], writes=[pb[pbq]])
                p.op("scalar", lambda q, ht=ht, pbq=pbq: q.activation(
                        fsq, PB[pbq][:, 0:512], AF.Relu),
                     reads=[pb[pbq]], writes=[buf("fsq")])
                p.op("vector", lambda q, ht=ht: q.tensor_tensor(
                        out=Har[:, ht * 512:(ht + 1) * 512], in0=fsq, in1=fsq, op=ALU.mult),
                     reads=[buf("fsq")], writes=[buf("Har")])
                p.op("vector", lambda q, ht=ht: q.tensor_tensor(
                        out=fsq, in0=Har[:, ht * 512:(ht + 1) * 512],
                        in1=Har[:, ht * 512:(ht + 1) * 512], op=ALU.mult),
                     reads=[buf("Har")], writes=[buf("fsq")])
                p.op("tensor", lambda q, ht=ht: q.matmul(
                        PB[4][0:1, 0:512], cOnesColB[:], fsq, start=(ht == 0), stop=(ht == 63)),
                     reads=[buf("fsq"), buf("onescolb")], writes=[pb[4]])
            rr = rows[0:1, S + B0:S + B0 + 512]
            sc0 = rows[0:1, 3 * S:3 * S + 512]
            sc1 = rows[0:1, 3 * S + 512:3 * S + 1024]
            sc2 = rows[0:1, 3 * S + 1024:3 * S + 1536]
            p.op("vector", lambda q: q.tensor_scalar(out=sc0, in0=PB[4][0:1, 0:512], scalar1=1.0 / HID, scalar2=None, op0=ALU.mult),
                 reads=[pb[4], buf("rows")], writes=[buf("rows")])
            p.op("vector", lambda q, rr=rr: q.tensor_tensor(out=sc1, in0=rr, in1=rr, op=ALU.mult),
                 reads=[buf("rows")], writes=[buf("rows")])
            p.op("vector", lambda q: q.tensor_tensor(out=sc2, in0=sc1, in1=sc1, op=ALU.mult),
                 reads=[buf("rows")], writes=[buf("rows")])
            p.op("vector", lambda q: q.tensor_tensor(out=sc0, in0=sc0, in1=sc2, op=ALU.mult),
                 reads=[buf("rows")], writes=[buf("rows")])
            p.op("scalar", lambda q: q.activation(sc0, sc0, AF.Sqrt, bias=cb1(RMSEPS)),
                 reads=[buf("rows")], writes=[buf("rows")])
            p.op("vector", lambda q: q.reciprocal(sc2, sc0),
                 reads=[buf("rows")], writes=[buf("rows")])
            p.op("vector", lambda q: q.tensor_tensor(out=r_cffn, in0=sc2, in1=sc1, op=ALU.mult),
                 reads=[buf("rows")], writes=[buf("rows")])
            bcast_row(RXB[:, 0:512], r_cffn, "RXBc", 512)
            for dt in range(16):
                wst = w2st0
                bws = buf("w2s0")
                for a0 in range(0, 64, 16):
                    p.op("gpsimd", lambda q, dt=dt, a0=a0, wst=wst: q.dma_start(
                            out=wst[:, a0 * 128:(a0 + 16) * 128].rearrange("p (a c) -> p a c", c=128),
                            in_=w2t_s.rearrange("(a pp) d -> pp a d", pp=128)[:, a0:a0 + 16, dt * 128:(dt + 1) * 128]),
                         reads=[buf("attnb"), buf("w2t")], writes=[bws, buf("w1s0"), buf("w1s1")], dma=True)
                pbq = 2 + dt % 2
                for h in range(64):
                    p.op("tensor", lambda q, h=h, dt=dt, wst=wst, pbq=pbq: q.matmul(
                            PB[pbq][:, 0:512], wst[:, h * 128:(h + 1) * 128],
                            Har[:, h * 512:(h + 1) * 512],
                            start=(h == 0), stop=(h == 63)),
                         reads=[bws, buf("Har")], writes=[pb[pbq]])
                p.op("vector", lambda q, dt=dt, pbq=pbq: q.tensor_tensor(
                        out=ffnb[:, dt * 512:(dt + 1) * 512], in0=PB[pbq][:, 0:512],
                        in1=RXB[:, 0:512], op=ALU.mult),
                     reads=[pb[pbq], buf("RXBc")], writes=[buf("x1stage")])
                p.op("tensor", lambda q, dt=dt: q.matmul(
                        PB[5][0:1, 0:512], cOnesColB[:], ffnb[:, dt * 512:(dt + 1) * 512],
                        start=(dt == 0), stop=(dt == 15)),
                     reads=[buf("x1stage"), buf("onescolb")], writes=[pb[5]])
                p.op("scalar", lambda q, dt=dt: q.activation(fsq, ffnb[:, dt * 512:(dt + 1) * 512], AF.Square),
                     reads=[buf("x1stage")], writes=[buf("fsq")])
                p.op("tensor", lambda q, dt=dt: q.matmul(
                        PB[6][0:1, 0:512], cOnesColB[:], fsq, start=(dt == 0), stop=(dt == 15)),
                     reads=[buf("fsq"), buf("onescolb")], writes=[pb[6]])
            r3_rstd = rows[0:1, 3 * S + 1536:3 * S + 2048]
            r3_negmr = rows[0:1, 3 * S + 2048:3 * S + 2560]
            rowstats([(PB[5][0:1, 0:512], pb[5])], [(PB[6][0:1, 0:512], pb[6])],
                     512, r3_rstd, r3_negmr, None, LNEPS, D)
            bcast_row(RXB[:, 0:512], r3_rstd, "RXB3", 512)
            bcast_row(RXB[:, 512:1024], r3_negmr, "RXB4", 512)
            for dt in range(16):
                p.op("sync", lambda q, dt=dt, B0=B0: q.dma_start(
                        out=fx, in_=x1f_s[dt * 128:(dt + 1) * 128, B0:B0 + 512]),
                     reads=[buf("w1s0"), buf("w1s1")], writes=[buf("fx"), buf("Har")], dma=True)
                p.op("vector", lambda q, dt=dt: q.tensor_tensor(
                        out=fx, in0=fx, in1=RXB[:, 512:1024], op=ALU.add),
                     reads=[buf("fx"), buf("RXB4")], writes=[buf("fx")])
                p.op("vector", lambda q, dt=dt: q.tensor_tensor(
                        out=fsq, in0=ffnb[:, dt * 512:(dt + 1) * 512], in1=RXB[:, 0:512], op=ALU.mult),
                     reads=[buf("x1stage"), buf("RXB3")], writes=[buf("fsq")])
                p.op("vector", lambda q, dt=dt: q.tensor_tensor(
                        out=fx, in0=fx, in1=fsq, op=ALU.add),
                     reads=[buf("fsq"), buf("fx")], writes=[buf("fx")])
                p.op("sync", lambda q, dt=dt, B0=B0: q.dma_start(
                        out=outT[dt * 128:(dt + 1) * 128, B0:B0 + 512], in_=fx),
                     reads=[buf("fx"), buf("Har")], writes=[buf("outT")], dma=True)

        p.op("vector", lambda q: q.engine_nop(), reads=[buf("outT")])

        with nc.Block() as block:
            p.replay(block)

    return nc


_NC_CACHE = None

def kernel(**inputs):
    global _NC_CACHE
    x = np.asarray(inputs["L0_attn_hidden_states"], dtype=np.float32)
    Wq = np.ascontiguousarray(np.asarray(inputs["L0_attn_W_qkvz"], dtype=np.float32))
    Wb = np.asarray(inputs["L0_attn_W_ba"], dtype=np.float32)
    ck = np.asarray(inputs["L0_attn_conv_kernel"], dtype=np.float32)
    Al = np.asarray(inputs["L0_attn_A_log"], dtype=np.float32)
    dtbv = np.asarray(inputs["L0_attn_dt_bias"], dtype=np.float32)
    nwv = np.asarray(inputs["L0_attn_norm_weight"], dtype=np.float32)
    Wo = np.ascontiguousarray(np.asarray(inputs["L0_attn_W_out"], dtype=np.float32))
    Wf = np.ascontiguousarray(np.asarray(inputs["L0_ffn_W_fc"], dtype=np.float32))
    Wp = np.ascontiguousarray(np.asarray(inputs["L0_ffn_W_proj"], dtype=np.float32))

    perm = [4 * kh + j for kh in range(16) for j in range(2)] + \
           [4 * kh + 2 + j for kh in range(16) for j in range(2)]
    Wb_p = np.ascontiguousarray(Wb[:, perm])
    ckT = np.ascontiguousarray(ck[:, 0, :].T)

    if _NC_CACHE is None:
        _NC_CACHE = build_program()
    nc = _NC_CACHE

    in_maps = []
    for b in range(8):
        in_maps.append({
            "xT": np.ascontiguousarray(x[b].T),
            "Wqkvz": Wq,
            "Wba": Wb_p,
            "convkT": ckT,
            "Alog": np.ascontiguousarray(Al.reshape(32, 1)),
            "dtbI": np.ascontiguousarray(dtbv.reshape(32, 1)),
            "nwIn": np.ascontiguousarray(nwv.reshape(1, DV)),
            "Wout": Wo,
            "Wfc": Wf,
            "Wproj": Wp,
        })
    res = run_bass_kernel_spmd(nc, in_maps, core_ids=list(range(8)))
    out = np.empty((8, S, D), dtype=np.float32)
    for b in range(8):
        out[b] = res.results[b]["outT"].T
    return out
